# version 13 (final)
# baseline (speedup 1.0000x reference)
import math
import time
import numpy as np
import ml_dtypes
import concourse.bass as bass
import concourse.mybir as mybir
from concourse.bass_utils import run_bass_kernel_spmd
from concourse.bass_core_rust import IndirectOffsetOnAxis

F32 = mybir.dt.float32
BF16 = mybir.dt.bfloat16
U32 = mybir.dt.uint32
AF = mybir.ActivationFunctionType
ALU = mybir.AluOpType
AX = mybir.AxisListType
ET = mybir.EngineType

ROLE_G = [
    [0, 31, 2, 29, 4, 27, 6, 25, 8, 23, 10, 21, 12, 19, 14, 17],
    [1, 30, 3, 28, 5, 26, 7, 24, 9, 22, 11, 20, 13, 18, 15, 16],
]
NPAD = [s + 2 if s % 2 == 0 else 33 - s for s in range(16)]
NCH = [(n + 3) // 4 for n in NPAD]
REG0 = [NPAD[s] - 2 for s in range(16)]
WTIL = [4 * NCH[s] - NPAD[s] + 2 for s in range(16)]

D = 2048
T = 2048
NTT = 16
HID = 8192
RMS_EPS = 1.1920929e-07
LN_EPS = 1e-5
INV_SQRT_HD = 1.0 / math.sqrt(128.0)
bfdt = ml_dtypes.bfloat16


class Sem:
    def __init__(self, h):
        self.h = h
        self.c = 0


class Ph:

    def __init__(self, nc):
        self.nc = nc
        self.L = {"tensor": [], "vector": [], "scalar": [], "sync": [], "gpsimd": []}

    def dma(self, eng, out_ap, in_ap, sem, **kw):
        tgt = sem.c + 16

        def f(q, o=out_ap, i=in_ap, h=sem.h, kw=kw):
            q.dma_start(out=o, in_=i, **kw).then_inc(h, 16)
        self.L[eng].append(f)
        sem.c = tgt
        return tgt

    def idma(self, eng, out_ap, out_off, in_ap, in_off, sem):
        tgt = sem.c + 16

        def f(q, o=out_ap, oo=out_off, i=in_ap, io=in_off, h=sem.h):
            q.indirect_dma_start(out=o, out_offset=oo, in_=i,
                                 in_offset=io).then_inc(h, 16)
        self.L[eng].append(f)
        sem.c = tgt
        return tgt

    def wait(self, eng, sem, val=None):
        v = sem.c if val is None else val
        if v <= 0:
            return

        def f(q, h=sem.h, v=v):
            q.wait_ge(h, v)
        self.L[eng].append(f)

    def op(self, eng, sem, fn):
        if eng in ("vector", "scalar"):
            prev = sem.c

            def fw(q, h=sem.h, v=prev):
                if v > 0:
                    q.wait_ge(h, v)
            self.L[eng].append(fw)
        tgt = sem.c + 1

        def f(q, fn=fn, h=sem.h):
            fn(q).then_inc(h, 1)
        self.L[eng].append(f)
        sem.c = tgt
        return tgt

    def opn(self, eng, fn):
        self.L[eng].append(fn)

    def run(self):
        with self.nc.Block() as blk:
            blk.tensor(lambda e: [f(e) for f in self.L["tensor"]])
            blk.vector(lambda e: [f(e) for f in self.L["vector"]])
            blk.scalar(lambda e: [f(e) for f in self.L["scalar"]])
            blk.sync(lambda e: [f(e) for f in self.L["sync"]])
            blk.gpsimd(lambda e: [f(e) for f in self.L["gpsimd"]])


RACE_DETECT = [True]
DEBUG_PRIVATE = [False]


def build_nc(stop="full"):
    nc = bass.Bass("TRN2", target_bir_lowering=False,
                   detect_race_conditions=RACE_DETECT[0])

    def din(name, shape, dt=F32):
        return nc.dram_tensor(name, shape, dt, kind="ExternalInput")

    x_in = din("x_in", [T, D])
    wgT = din("wgT", [64 * D, 128])
    wuT = din("wuT", [64 * D, 128])
    wdT = din("wdT", [HID, D])
    wqkvT = din("wqkvT", [D, 2 * D])
    woT_in = din("woT", [D, D])
    acombT = din("acombT", [D, 96])
    bcombT = din("bcombT", [96, 2 * D])
    aoT_in = din("aoT", [D, 32])
    boT_in = din("boT", [32, D])
    svec_in = din("svec", [1, 32])
    ident_b_in = din("ident_b", [128, 128], BF16)
    cos4_in = din("cos4", [T, 128])
    sin4_in = din("sin4", [T, 128])
    masks_in = din("masks", [16, 128, 512], BF16)
    kidx_in = din("kidx", [128, 16], U32)
    vidx_in = din("vidx", [128, 16], U32)
    fidx_in = din("fidx", [2, 1], U32)
    tok_in = din("tok", [2, 64], U32)
    ones_b_in = din("ones_b", [1, 128], BF16)

    y_out = nc.dram_tensor("y_out", [T, D], F32, kind="ExternalOutput")

    lnx1T = nc.dram_tensor("lnx1T", [16, 128, T], BF16)
    lnx2T = nc.dram_tensor("lnx2T", [16, 128, T], BF16)
    lnx3T = nc.dram_tensor("lnx3T", [16, 128, T], BF16)
    x1sp = nc.dram_tensor("x1sp", [T, D], F32)
    x2sp = nc.dram_tensor("x2sp", [T, D], F32)
    qtsp = nc.dram_tensor("qtsp", [16, 16, 128, 128], BF16)

    _spc = "Local" if DEBUG_PRIVATE[0] else "Shared"
    kt_sh = nc.dram_tensor("kt_sh", [4096, 1024], BF16, addr_space=_spc)
    v_sh = nc.dram_tensor("v_sh", [4096, 1024], BF16, addr_space=_spc)
    flag_sh = nc.dram_tensor("flag_sh", [4, 64], U32, addr_space=_spc)
    wg_bf = nc.dram_tensor("wg_bf", [128, 131072], BF16)
    wu_bf = nc.dram_tensor("wu_bf", [128, 131072], BF16)
    wd_bf = nc.dram_tensor("wd_bf", [128, 131072], BF16)

    BASE = 0x4080
    _A = nc.alloc_sbuf_tensor_at
    def A(name, shape, dt, offset):
        return _A(name, shape, dt, offset=BASE + offset)
    ident_b = A("ident_b_s", [128, 128], BF16, offset=0x00000)
    sb_all = A("sb_all", [128, 32], F32, offset=0x00300)
    kidx = A("kidx_s", [128, 16], U32, offset=0x00380)
    vidx = A("vidx_s", [128, 16], U32, offset=0x003C0)
    cols = A("cols", [128, 64], F32, offset=0x00400)
    flagv = A("flagv", [4, 64], U32, offset=0x00500)
    tokrow = A("tokrow", [2, 64], U32, offset=0x00600)
    fidx = A("fidx_s", [2, 1], U32, offset=0x00700)
    svec_s = A("svec_s", [1, 32], F32, offset=0x00720)
    svec_b = A("svec_b", [1, 32], BF16, offset=0x007A0)
    ones_b = A("ones_b_s", [1, 128], BF16, offset=0x007E0)
    c_lneps = A("c_lneps", [128, 1], F32, offset=0x00900)
    c_rmseps = A("c_rmseps", [128, 1], F32, offset=0x00920)
    c_neghalf = A("c_neghalf", [128, 1], F32, offset=0x00940)
    c_inv128 = A("c_inv128", [128, 1], F32, offset=0x00960)
    lnstage = A("lnstage", [128, 16 * 128], BF16, offset=0x2EA00)

    hid_rot = A("hid_rot", [128, 8 * 1024], BF16, offset=0x00A00)
    yacc = A("yacc", [128, 8 * 2048], F32, offset=0x04A00)
    lnx_ffn = A("lnx_ffn", [128, 16 * 1024], BF16, offset=0x14A00)
    wd2 = [A("wd2a", [128, 8 * 1024], BF16, offset=0x1CA00),
           A("wd2b", [128, 8 * 1024], BF16, offset=0x20A00)]
    wgb = [A("wgba", [128, 16 * 128], BF16, offset=0x24A00),
           A("wgbb", [128, 16 * 128], BF16, offset=0x25A00)]
    wub = [A("wuba", [128, 16 * 128], BF16, offset=0x26A00),
           A("wubb", [128, 16 * 128], BF16, offset=0x27A00)]
    x0b = [A("x0ba", [128, 2048], F32, offset=0x28A00),
           A("x0bb", [128, 2048], F32, offset=0x2AA00)]
    lnb = A("lnb", [128, 2048], BF16, offset=0x2CA00)
    sqj = A("sqj", [128, 2048], BF16, offset=0x2DA00)

    wqkv_s = A("wqkv_s", [128, 16 * 4096], BF16, offset=0x00A00)
    cos4 = A("cos4_s", [128, 16 * 128], F32, offset=0x20A00)
    sin4 = A("sin4_s", [128, 16 * 128], F32, offset=0x22A00)
    acomb_s = A("acomb_s", [128, 16 * 96], BF16, offset=0x24A00)
    bcomb_s = A("bcomb_s", [96, 4096], BF16, offset=0x25800)
    lnq = [A("lnqa", [128, 16 * 128], BF16, offset=0x27800),
           A("lnqb", [128, 16 * 128], BF16, offset=0x28800)]
    qf4 = A("qf4", [128, 2048], F32, offset=0x29800)
    qrope = A("qrope", [128, 512], F32, offset=0x2B800)
    sqv = A("sqv", [128, 512], F32, offset=0x2C000)
    ktst = A("ktst", [128, 1024], BF16, offset=0x2C800)
    vst = A("vst", [128, 1024], BF16, offset=0x2D000)
    qtst = A("qtst", [128, 512], BF16, offset=0x2D800)
    qstage = A("qstage", [128, 512], BF16, offset=0x2DC00)
    zb = A("zb", [128, 96], BF16, offset=0x2E000)
    zTt = A("zTt", [96, 128], BF16, offset=0x2E200)

    yT_at = A("yT_at", [128, 16 * 2048], BF16, offset=0x00A00)
    masks_s = A("masks_s", [128, 16 * 512], BF16, offset=0x10A00)
    s_sb2 = [A("s_sba", [128, 4096], F32, offset=0x15A00),
             A("s_sbb", [128, 4096], F32, offset=0x19A00)]
    p_sb2 = [A("p_sba", [128, 4096], BF16, offset=0x1DA00),
             A("p_sbb", [128, 4096], BF16, offset=0x1FA00)]
    pt_sb2 = [A("pt_sba", [128, 32 * 128], BF16, offset=0x21A00),
              A("pt_sbb", [128, 32 * 128], BF16, offset=0x23A00)]
    ktb = [A("ktba", [128, 4096], BF16, offset=0x25A00),
           A("ktbb", [128, 4096], BF16, offset=0x27A00)]
    vbuf = [A("vba", [128, 4096], BF16, offset=0x29A00),
            A("vbb", [128, 4096], BF16, offset=0x2BA00)]
    qtb = [A("qtba", [128, 16 * 128], BF16, offset=0x2DA00),
           A("qtbb", [128, 16 * 128], BF16, offset=0x2EA00)]

    wo_s = A("wo_s", [128, 16 * 2048], BF16, offset=0x10A00)
    aoT_s = A("aoT_s", [128, 16 * 32], BF16, offset=0x20A00)
    boT_s = A("boT_s", [32, 2048], BF16, offset=0x20E00)
    x1rd = [A("x1rda", [128, 2048], F32, offset=0x21E00),
            A("x1rdb", [128, 2048], F32, offset=0x23E00)]
    x2t = A("x2t", [128, 2048], F32, offset=0x25E00)
    zob = A("zob", [128, 32], BF16, offset=0x27E00)
    zoT = A("zoT", [32, 128], BF16, offset=0x27F00)
    lnb3 = A("lnb3", [128, 2048], BF16, offset=0x28000)
    sqj3 = A("sqj3", [128, 2048], BF16, offset=0x29000)

    pf = [nc.alloc_psum_tensor(f"pf{i}", [128, 512], F32) for i in range(8)]

    sW = Sem(nc.alloc_semaphore("sW"))
    sX = Sem(nc.alloc_semaphore("sX"))
    sG = Sem(nc.alloc_semaphore("sG"))
    sONE = Sem(nc.alloc_semaphore("sONE"))
    sONEG = Sem(nc.alloc_semaphore("sONEG"))
    sLNS = Sem(nc.alloc_semaphore("sLNS"))
    sLNX = Sem(nc.alloc_semaphore("sLNX"))
    sSPILL = Sem(nc.alloc_semaphore("sSPILL"))
    sQSP = Sem(nc.alloc_semaphore("sQSP"))
    sX0 = [Sem(nc.alloc_semaphore("sX0a")), Sem(nc.alloc_semaphore("sX0b"))]
    sXR = [Sem(nc.alloc_semaphore("sXRa")), Sem(nc.alloc_semaphore("sXRb"))]
    sWG = [Sem(nc.alloc_semaphore("sWGa")), Sem(nc.alloc_semaphore("sWGb"))]
    sWD = [Sem(nc.alloc_semaphore("sWDa")), Sem(nc.alloc_semaphore("sWDb"))]
    sLQ = [Sem(nc.alloc_semaphore("sLQa")), Sem(nc.alloc_semaphore("sLQb"))]
    sKT = [Sem(nc.alloc_semaphore("sKTa")), Sem(nc.alloc_semaphore("sKTb"))]
    sVB = [Sem(nc.alloc_semaphore("sVBa")), Sem(nc.alloc_semaphore("sVBb"))]
    sQT = [Sem(nc.alloc_semaphore("sQTa")), Sem(nc.alloc_semaphore("sQTb"))]
    sPE = Sem(nc.alloc_semaphore("sPE"))
    sDVE = Sem(nc.alloc_semaphore("sDVE"))
    sACT = Sem(nc.alloc_semaphore("sACT"))
    sSpin = Sem(nc.alloc_semaphore("sSpin"))
    sPoll = Sem(nc.alloc_semaphore("sPoll"))
    sOut = Sem(nc.alloc_semaphore("sOut"))
    sPL = Sem(nc.alloc_semaphore("sPL"))
    sWGS = Sem(nc.alloc_semaphore("sWGS"))
    sWDS = Sem(nc.alloc_semaphore("sWDS"))

    P = Ph(nc)
    P.dma("sync", ident_b[:, :], ident_b_in.ap(), sONE)
    P.dma("sync", kidx[:, :], kidx_in.ap(), sONE)
    P.dma("sync", vidx[:, :], vidx_in.ap(), sONE)
    P.dma("sync", svec_s[:, :], svec_in.ap(), sONE)
    P.dma("sync", tokrow[:, :], tok_in.ap(), sONE)
    P.dma("sync", fidx[:, :], fidx_in.ap(), sONE)
    P.dma("sync", ones_b[:, :], ones_b_in.ap(), sONE)
    P.wait("vector", sONE)
    P.op("vector", sDVE, lambda v: v.memset(flagv[:, :], 0))
    P.op("vector", sDVE, lambda v: v.memset(c_lneps[:, :], LN_EPS))
    P.op("vector", sDVE, lambda v: v.memset(c_rmseps[:, :], RMS_EPS))
    P.op("vector", sDVE, lambda v: v.memset(c_neghalf[:, :], -0.5))
    P.op("vector", sDVE, lambda v: v.memset(c_inv128[:, :], 1.0 / 128.0))
    P.op("vector", sDVE, lambda v: v.tensor_copy(out=svec_b[:, :], in_=svec_s[:, :]))
    P.wait("tensor", sDVE)
    P.op("tensor", sPE, lambda t: t.matmul(pf[7][0:128, 0:32], ones_b[0:1, :],
                                           svec_b[0:1, :], start=True, stop=True))
    P.wait("vector", sPE)
    P.op("vector", sDVE, lambda v: v.tensor_copy(out=sb_all[:, :],
                                                 in_=pf[7][0:128, 0:32]))
    P.op("vector", sDVE, lambda v: v.tensor_scalar(
        out=sb_all[:, 4:20], in0=sb_all[:, 4:20], scalar1=INV_SQRT_HD,
        scalar2=None, op0=ALU.mult))
    P.run()

    def finisher():
        P = Ph(nc)
        for tt in range(NTT):
            P.dma("sync", y_out.ap()[tt * 128:(tt + 1) * 128, :],
                  x_in.ap()[tt * 128:(tt + 1) * 128, :], sOut)
        P.wait("sync", sOut)
        P.run()
        return nc

    if stop == "p0":
        return finisher()

    TRC = [0]
    LNSTORE = [0]

    def emit_ln_tile(P, xtile, lnbuf, sqbuf, scratch, tt):
        cm, cs, cv = 0, 1, 2
        m1 = P.op("vector", sDVE,
                  lambda v: v.reduce_sum(cols[:, cm:cm + 1], xtile, axis=AX.X))
        P.wait("scalar", sDVE, m1)
        a1 = P.op("scalar", sACT, lambda s: s.activation(
            out=sqbuf[:, :], in_=xtile, func=AF.Square,
            accum_out=cols[:, cs:cs + 1]))
        P.wait("vector", sACT, a1)
        P.op("vector", sDVE, lambda v: v.tensor_scalar(
            out=cols[:, cm:cm + 1], in0=cols[:, cm:cm + 1], scalar1=1.0 / D,
            scalar2=None, op0=ALU.mult))
        P.op("vector", sDVE, lambda v: v.tensor_scalar(
            out=cols[:, cs:cs + 1], in0=cols[:, cs:cs + 1], scalar1=1.0 / D,
            scalar2=None, op0=ALU.mult))
        P.op("vector", sDVE, lambda v: v.tensor_tensor(
            out=cols[:, cv:cv + 1], in0=cols[:, cm:cm + 1],
            in1=cols[:, cm:cm + 1], op=ALU.mult))
        v1 = P.op("vector", sDVE, lambda v: v.tensor_tensor(
            out=cols[:, cv:cv + 1], in0=cols[:, cs:cs + 1],
            in1=cols[:, cv:cv + 1], op=ALU.subtract))
        P.wait("scalar", sDVE, v1)
        P.op("scalar", sACT, lambda s: s.activation(
            out=cols[:, cv:cv + 1], in_=cols[:, cv:cv + 1], func=AF.Ln,
            bias=c_lneps[:, :]))
        a2 = P.op("scalar", sACT, lambda s: s.activation(
            out=cols[:, cv:cv + 1], in_=cols[:, cv:cv + 1], func=AF.Exp,
            scale=c_neghalf[:, :]))
        P.wait("vector", sACT, a2)
        lnx_done = P.op("vector", sDVE, lambda v: v.tensor_scalar(
            out=lnbuf[:, :], in0=xtile, scalar1=cols[:, cm:cm + 1],
            scalar2=cols[:, cv:cv + 1], op0=ALU.subtract, op1=ALU.mult))
        P.wait("vector", sLNS, LNSTORE[0])
        for half in range(2):
            bank = pf[6 + (TRC[0] % 2)]
            TRC[0] += 1
            pb = bank[:, :].bitcast(BF16)
            P.wait("tensor", sDVE, lnx_done)
            for j in range(8):
                ic = half * 8 + j
                if j < 7:
                    P.opn("tensor", lambda t, pb=pb, j=j, ic=ic: t.transpose(
                        pb[:, j * 128:(j + 1) * 128],
                        lnbuf[:, ic * 128:(ic + 1) * 128], ident_b[:, :]))
                else:
                    tr = P.op("tensor", sPE, lambda t, pb=pb, j=j, ic=ic: t.transpose(
                        pb[:, j * 128:(j + 1) * 128],
                        lnbuf[:, ic * 128:(ic + 1) * 128], ident_b[:, :]))
            P.wait("vector", sPE, tr)
            P.op("vector", sDVE, lambda v, pb=pb, half=half: v.tensor_copy(
                out=lnstage[:, half * 1024:(half + 1) * 1024], in_=pb[:, :]))
        P.wait("sync", sDVE)
        dst = scratch.ap()[:, :, tt * 128:(tt + 1) * 128].rearrange("ic p t -> p ic t")
        LNSTORE[0] = P.dma("sync", dst,
                           lnstage[:, :].rearrange("p (ic t) -> p ic t", t=128), sLNS)

    P = Ph(nc)
    for tt in range(NTT):
        buf = x0b[tt % 2]
        P.wait("scalar", sDVE)
        ld = P.dma("scalar", buf[:, :], x_in.ap()[tt * 128:(tt + 1) * 128, :],
                   sX0[tt % 2])
        P.wait("vector", sX0[tt % 2], ld)
        P.wait("scalar", sX0[tt % 2], ld)
        emit_ln_tile(P, buf[:, :], lnb, sqj, lnx1T, tt)

    if stop == "ln1":
        P.run()
        return finisher()

    def emit_ffn(P, lnsrc, resid, final, max_tc=2, max_hb=8, do_down=True, do_tail=True, do_mm=True, do_act=True, modes=("cast", "cast")):
        wg_last = [0, 0]
        evac_last = [0, 0]
        evac_any = [0]
        down_mm_end = 0
        down_evac_end = 0
        evac_done_hb = 0
        wgstore = [0, 0]
        wdstore = [0, 0]
        for tc in range(max_tc):
            mode = modes[tc] if tc < len(modes) else "read"
            P.wait("sync", sPE)
            P.wait("sync", sLNS)
            lnld = P.dma("sync",
                         lnx_ffn[:, :].rearrange("p (ic t) -> p ic t", t=1024),
                         lnsrc.ap()[:, :, tc * 1024:(tc + 1) * 1024]
                         .rearrange("ic p t -> p ic t"), sLNX)
            for hb in range(max_hb):
                for otl in range(8):
                    ot = hb * 8 + otl
                    gb_, ub_ = wgb[ot % 2], wub[ot % 2]
                    if mode in ("write", "cast"):
                        P.wait("gpsimd", sPE, wg_last[ot % 2])
                        if mode == "write":
                            P.wait("gpsimd", sWGS, wgstore[ot % 2])
                        lw = P.dma("gpsimd",
                                   gb_[:, :].rearrange("p (ic o) -> p ic o", o=128),
                                   wgT.ap()[ot * 2048:(ot + 1) * 2048, :]
                                   .rearrange("(ic p) o -> p ic o", p=128),
                                   sWG[ot % 2])
                        lw2 = P.dma("gpsimd",
                                    ub_[:, :].rearrange("p (ic o) -> p ic o", o=128),
                                    wuT.ap()[ot * 2048:(ot + 1) * 2048, :]
                                    .rearrange("(ic p) o -> p ic o", p=128),
                                    sWG[ot % 2])
                        if mode == "write":
                            P.wait("sync", sWG[ot % 2], lw2)
                            P.dma("sync", wg_bf.ap()[:, ot * 2048:(ot + 1) * 2048],
                                  gb_[:, :], sWGS)
                            P.wait("scalar", sWG[ot % 2], lw2)
                            wgstore[ot % 2] = P.dma(
                                "scalar", wu_bf.ap()[:, ot * 2048:(ot + 1) * 2048],
                                ub_[:, :], sWGS)
                    else:
                        P.wait("sync", sPE, wg_last[ot % 2])
                        P.wait("scalar", sPE, wg_last[ot % 2])
                        lw = P.dma("sync", gb_[:, :],
                                   wg_bf.ap()[:, ot * 2048:(ot + 1) * 2048],
                                   sWG[ot % 2])
                        lw2 = P.dma("scalar", ub_[:, :],
                                    wu_bf.ap()[:, ot * 2048:(ot + 1) * 2048],
                                    sWG[ot % 2])
                    P.wait("tensor", sWG[ot % 2], lw2)
                    if ot == 0:
                        P.wait("tensor", sLNX, lnld)
                        P.wait("tensor", sDVE)
                    if otl == 0 and down_evac_end > 0:
                        P.wait("tensor", sDVE, down_evac_end)
                    P.wait("tensor", sDVE, evac_last[ot % 2])
                    b0 = (ot % 2) * 4
                    ga, gb2, ua, ub2 = pf[b0], pf[b0 + 1], pf[b0 + 2], pf[b0 + 3]
                    if not do_mm:
                        continue
                    for ic in range(16):
                        for ts in range(2):
                            P.opn("tensor", lambda t,
                                  ic=ic, ts=ts, gb_=gb_, ga=ga, gb2=gb2: t.matmul(
                                      (ga if ts == 0 else gb2)[:, :],
                                      gb_[:, ic * 128:(ic + 1) * 128],
                                      lnx_ffn[:, ic * 1024 + ts * 512:
                                              ic * 1024 + (ts + 1) * 512],
                                      start=(ic == 0), stop=(ic == 15)))
                        for ts in range(2):
                            def mmfn(t, ic=ic, ts=ts, ub_=ub_, ua=ua, ub2=ub2):
                                return t.matmul(
                                    (ua if ts == 0 else ub2)[:, :],
                                    ub_[:, ic * 128:(ic + 1) * 128],
                                    lnx_ffn[:, ic * 1024 + ts * 512:
                                            ic * 1024 + (ts + 1) * 512],
                                    start=(ic == 0), stop=(ic == 15))
                            if ic == 15 and ts == 1:
                                wg_last[ot % 2] = P.op("tensor", sPE, mmfn)
                            else:
                                P.opn("tensor", mmfn)
                    if not do_act:
                        continue
                    P.wait("scalar", sPE, wg_last[ot % 2])
                    P.wait("scalar", sDVE, evac_any[0])
                    for ts in range(2):
                        P.op("scalar", sACT, lambda s, ts=ts, ga=ga, gb2=gb2:
                             s.activation(out=lnb[:, ts * 512:(ts + 1) * 512],
                                          in_=(ga if ts == 0 else gb2)[:, :],
                                          func=AF.Silu))
                    P.wait("vector", sACT)
                    if down_mm_end > 0 and otl == 0:
                        P.wait("vector", sPE, down_mm_end)
                    for ts in range(2):
                        ev = P.op("vector", sDVE, lambda v, ts=ts, otl=otl, ua=ua,
                                  ub2=ub2: v.tensor_tensor(
                                      out=hid_rot[:, otl * 1024 + ts * 512:
                                                  otl * 1024 + (ts + 1) * 512],
                                      in0=lnb[:, ts * 512:(ts + 1) * 512],
                                      in1=(ua if ts == 0 else ub2)[:, :],
                                      op=ALU.mult))
                    evac_last[ot % 2] = ev
                    evac_any[0] = ev
                    evac_done_hb = ev
                if not do_down:
                    continue
                wdld = [0, 0]
                for half in range(2):
                    qn = "sync" if half == 0 else "scalar"
                    if mode in ("write", "cast"):
                        P.wait("gpsimd", sPE, down_mm_end)
                        if mode == "write":
                            P.wait("gpsimd", sWDS, wdstore[half])
                        wdld[half] = P.dma(
                            "gpsimd",
                            wd2[half][:, :].rearrange("p (hc d) -> p hc d", d=2048),
                            wdT.ap()[hb * 1024 + half * 512:
                                     hb * 1024 + (half + 1) * 512, :]
                            .rearrange("(hc p) d -> p hc d", p=128), sWD[half])
                        if mode == "write":
                            P.wait(qn, sWD[half], wdld[half])
                            wdstore[half] = P.dma(
                                qn, wd_bf.ap()[:, hb * 16384 + half * 8192:
                                               hb * 16384 + (half + 1) * 8192],
                                wd2[half][:, :], sWDS)
                    else:
                        P.wait(qn, sPE, down_mm_end)
                        wdld[half] = P.dma(
                            qn, wd2[half][:, :],
                            wd_bf.ap()[:, hb * 16384 + half * 8192:
                                       hb * 16384 + (half + 1) * 8192], sWD[half])
                P.wait("tensor", sWD[0], wdld[0])
                P.wait("tensor", sWD[1], wdld[1])
                P.wait("tensor", sDVE, evac_done_hb)
                if down_evac_end > 0:
                    P.wait("tensor", sDVE, down_evac_end)
                down_evac_par = getattr(emit_ffn, "_dep", None)
                if down_evac_par is None:
                    down_evac_par = [0, 0]
                    emit_ffn._dep = down_evac_par
                for tt in range(8):
                    b0 = (tt % 2) * 4
                    P.wait("tensor", sDVE, down_evac_par[tt % 2])
                    mme = None
                    for hc in range(8):
                        wdb = wd2[hc // 4]
                        hcl = hc % 4
                        for dc in range(4):
                            def mmfn(t, tt=tt, hc=hc, hcl=hcl, wdb=wdb,
                                     dc=dc, b0=b0):
                                return t.matmul(
                                    pf[b0 + dc][:, :],
                                    hid_rot[:, hc * 1024 + tt * 128:
                                            hc * 1024 + (tt + 1) * 128],
                                    wdb[:, hcl * 2048 + dc * 512:
                                        hcl * 2048 + (dc + 1) * 512],
                                    start=(hc == 0), stop=(hc == 7))
                            if hc == 7 and dc == 3:
                                mme = P.op("tensor", sPE, mmfn)
                            else:
                                P.opn("tensor", mmfn)
                    P.wait("vector", sPE, mme)
                    for dc in range(4):
                        if hb == 0:
                            de = P.op("vector", sDVE, lambda v, tt=tt, dc=dc,
                                      b0=b0: v.tensor_copy(
                                          out=yacc[:, tt * 2048 + dc * 512:
                                                   tt * 2048 + (dc + 1) * 512],
                                          in_=pf[b0 + dc][:, :]))
                        else:
                            de = P.op("vector", sDVE, lambda v, tt=tt, dc=dc,
                                      b0=b0: v.tensor_tensor(
                                          out=yacc[:, tt * 2048 + dc * 512:
                                                   tt * 2048 + (dc + 1) * 512],
                                          in0=pf[b0 + dc][:, :],
                                          in1=yacc[:, tt * 2048 + dc * 512:
                                                   tt * 2048 + (dc + 1) * 512],
                                          op=ALU.add))
                    down_evac_end = de
                    down_evac_par[tt % 2] = de
                down_mm_end = mme
            if not do_tail:
                continue
            P.wait("vector", sPE, down_mm_end)
            for tt in range(8):
                ttg = tc * 8 + tt
                buf = x0b[tt % 2]
                P.wait("scalar", sDVE)
                P.wait("scalar", sSPILL)
                if final:
                    P.wait("scalar", sOut)
                ld = P.dma("scalar", buf[:, :],
                           resid.ap()[ttg * 128:(ttg + 1) * 128, :], sX0[tt % 2])
                P.wait("vector", sX0[tt % 2], ld)
                P.wait("vector", sDVE, down_evac_end)
                x1done = P.op("vector", sDVE, lambda v, tt=tt, buf=buf:
                              v.scalar_tensor_tensor(
                                  out=buf[:, :],
                                  in0=yacc[:, tt * 2048:(tt + 1) * 2048],
                                  scalar=0.5, in1=buf[:, :],
                                  op0=ALU.mult, op1=ALU.add))
                if final:
                    P.wait("scalar", sDVE, x1done)
                    P.dma("scalar", y_out.ap()[ttg * 128:(ttg + 1) * 128, :],
                          buf[:, :], sOut)
                else:
                    P.wait("scalar", sDVE, x1done)
                    P.dma("scalar", x1sp.ap()[ttg * 128:(ttg + 1) * 128, :],
                          buf[:, :], sSPILL)
                    emit_ln_tile(P, buf[:, :], lnb, sqj, lnx2T, ttg)
        if final:
            P.wait("scalar", sOut)

    if stop == "ffnload":
        emit_ffn(P, lnx1T, x_in, final=False, max_tc=1, max_hb=1,
                 do_down=False, do_tail=False, do_mm=False)
        P.run()
        return finisher()
    if stop == "ffnmm":
        emit_ffn(P, lnx1T, x_in, final=False, max_tc=1, max_hb=1,
                 do_down=False, do_tail=False, do_act=False)
        P.run()
        return finisher()
    if stop == "ffngate":
        emit_ffn(P, lnx1T, x_in, final=False, max_tc=1, max_hb=1,
                 do_down=False, do_tail=False)
        P.run()
        return finisher()
    if stop == "ffndown":
        emit_ffn(P, lnx1T, x_in, final=False, max_tc=1, max_hb=1, do_tail=False)
        P.run()
        return finisher()
    if stop == "ffntc0":
        emit_ffn(P, lnx1T, x_in, final=True, max_tc=1)
        P.run()
        return nc
    emit_ffn(P, lnx1T, x_in, final=(stop == "ffn1"))
    P.run()

    if stop == "ffn1":
        return nc

    P = Ph(nc)
    P.wait("sync", sSPILL)
    P.wait("gpsimd", sSPILL)
    P.wait("vector", sSPILL)
    P.wait("scalar", sSPILL)
    wqld = P.dma("gpsimd", wqkv_s[:, :].rearrange("p (ic o) -> p ic o", o=4096),
                 wqkvT.ap().rearrange("(ic p) o -> p ic o", p=128), sONEG)
    acld = P.dma("gpsimd", acomb_s[:, :].rearrange("p (ic r) -> p ic r", r=96),
                 acombT.ap().rearrange("(ic p) r -> p ic r", p=128), sONEG)
    bcld = P.dma("gpsimd", bcomb_s[:, :], bcombT.ap(), sONEG)
    c4 = P.dma("sync", cos4[:, :].rearrange("p (tt c) -> p tt c", c=128),
               cos4_in.ap().rearrange("(tt p) c -> p tt c", p=128), sONE)
    s4 = P.dma("sync", sin4[:, :].rearrange("p (tt c) -> p tt c", c=128),
               sin4_in.ap().rearrange("(tt p) c -> p tt c", p=128), sONE)
    P.wait("vector", sONEG)
    fold_done = 0
    for ic in range(16):
        for oc in range(8):
            sj = 0 if oc < 4 else (1 if oc < 6 else 2)
            fold_done = P.op("vector", sDVE, lambda v, ic=ic, oc=oc, sj=sj:
                             v.tensor_scalar(
                                 out=wqkv_s[:, ic * 4096 + oc * 512:
                                            ic * 4096 + (oc + 1) * 512],
                                 in0=wqkv_s[:, ic * 4096 + oc * 512:
                                            ic * 4096 + (oc + 1) * 512],
                                 scalar1=sb_all[:, sj:sj + 1], scalar2=None,
                                 op0=ALU.mult))
    P.wait("tensor", sONE)
    P.wait("tensor", sONEG)
    P.wait("tensor", sDVE, fold_done)

    scat_done = 0
    qdma_done = 0
    zc2_prev = [0]
    vst_prev = [0]
    trlast = [0]

    def process_q(P, oc, tt, slot4):
        src = qf4[:, (oc % 4) * 512:(oc % 4 + 1) * 512]
        P.wait("vector", sPE, trlast[0])
        P.wait("scalar", sDVE)
        sq1 = P.op("scalar", sACT, lambda s: s.activation(
            out=sqv[:, :], in_=src, func=AF.Square))
        P.wait("vector", sACT, sq1)
        P.op("vector", sDVE, lambda v: v.reduce_sum(
            cols[:, 8:12].rearrange("p (h o) -> p h o", o=1),
            sqv[:, :].rearrange("p (h c) -> p h c", c=128), axis=AX.X))
        P.wait("scalar", sDVE)
        P.op("scalar", sACT, lambda s: s.activation(
            out=cols[:, 8:12], in_=cols[:, 8:12], func=AF.Ln,
            scale=c_inv128[:, :], bias=c_rmseps[:, :]))
        rs = P.op("scalar", sACT, lambda s: s.activation(
            out=cols[:, 8:12], in_=cols[:, 8:12], func=AF.Exp,
            scale=c_neghalf[:, :]))
        P.wait("vector", sACT, rs)
        if oc < 4:
            P.op("vector", sDVE, lambda v: v.tensor_tensor(
                out=cols[:, 8:12], in0=cols[:, 8:12],
                in1=sb_all[:, 4 + 4 * oc:8 + 4 * oc], op=ALU.mult))
        xv = src.rearrange("p (h c) -> p h c", c=128)
        rv = qrope[:, :].rearrange("p (h c) -> p h c", c=128)
        sv = sqv[:, :].rearrange("p (h c) -> p h c", c=128)
        cosv = cos4[:, tt * 128:(tt + 1) * 128].rearrange("p (h c) -> p h c", c=32)
        sinv = sin4[:, tt * 128:(tt + 1) * 128].rearrange("p (h c) -> p h c", c=32)
        P.op("vector", sDVE, lambda v: v.tensor_tensor(
            out=rv[:, :, 0:32], in0=xv[:, :, 0:32], in1=cosv, op=ALU.mult))
        P.op("vector", sDVE, lambda v: v.tensor_tensor(
            out=sv[:, :, 0:32], in0=xv[:, :, 32:64], in1=sinv, op=ALU.mult))
        P.op("vector", sDVE, lambda v: v.tensor_tensor(
            out=rv[:, :, 0:32], in0=rv[:, :, 0:32], in1=sv[:, :, 0:32],
            op=ALU.add))
        P.op("vector", sDVE, lambda v: v.tensor_tensor(
            out=rv[:, :, 32:64], in0=xv[:, :, 32:64], in1=cosv, op=ALU.mult))
        P.op("vector", sDVE, lambda v: v.tensor_tensor(
            out=sv[:, :, 32:64], in0=xv[:, :, 0:32], in1=sinv, op=ALU.mult))
        P.op("vector", sDVE, lambda v: v.tensor_tensor(
            out=rv[:, :, 32:64], in0=rv[:, :, 32:64], in1=sv[:, :, 32:64],
            op=ALU.subtract))
        P.op("vector", sDVE, lambda v: v.tensor_copy(
            out=rv[:, :, 64:128], in_=xv[:, :, 64:128]))
        dstb = qtst if oc < 4 else None
        for h in range(4):
            P.op("vector", sDVE, lambda v, h=h: v.tensor_scalar(
                out=qtst[:, h * 128:(h + 1) * 128],
                in0=qrope[:, h * 128:(h + 1) * 128],
                scalar1=cols[:, 8 + h:9 + h], scalar2=None, op0=ALU.mult))
        tsdone = sDVE.c
        pb = pf[6][:, :].bitcast(BF16)
        P.wait("tensor", sDVE, tsdone)
        for h in range(4):
            tr = P.op("tensor", sPE, lambda t, h=h: t.transpose(
                pb[:, h * 128:(h + 1) * 128], qtst[:, h * 128:(h + 1) * 128],
                ident_b[:, :]))
        trlast[0] = tr
        P.wait("vector", sPE, tr)
        if oc < 4:
            P.wait("vector", sQSP)
            cp = P.op("vector", sDVE, lambda v: v.tensor_copy(
                out=qstage[:, :], in_=pb[:, 0:512]))
            P.wait("sync", sDVE, cp)
            return P.dma("sync",
                         qtsp.ap()[4 * oc:4 * oc + 4, tt, :, :]
                         .rearrange("h p t -> p h t"),
                         qstage[:, :].rearrange("p (h t) -> p h t", t=128), sQSP)
        else:
            koff = (oc - 4) * 512
            P.wait("vector", sG)
            return P.op("vector", sDVE, lambda v, koff=koff: v.tensor_copy(
                out=ktst[:, koff:koff + 512], in_=pb[:, 0:512]))

    P.wait("sync", sLNS)
    for tt in range(NTT):
        lbuf = lnq[tt % 2]
        P.wait("sync", sPE)
        lnld = P.dma("sync", lbuf[:, :].rearrange("p (ic t) -> p ic t", t=128),
                     lnx2T.ap()[:, :, tt * 128:(tt + 1) * 128]
                     .rearrange("ic p t -> p ic t"), sLQ[tt % 2])
        P.wait("tensor", sLQ[tt % 2], lnld)
        P.wait("tensor", sDVE, zc2_prev[0])
        P.wait("tensor", sACT, vst_prev[0])
        for ic in range(16):
            def mmfn(t, ic=ic, lbuf=lbuf):
                return t.matmul(
                    pf[4][:, 0:96], lbuf[:, ic * 128:(ic + 1) * 128],
                    acomb_s[:, ic * 96:(ic + 1) * 96],
                    start=(ic == 0), stop=(ic == 15))
            if ic == 15:
                zmm = P.op("tensor", sPE, mmfn)
            else:
                P.opn("tensor", mmfn)
        P.wait("vector", sPE, zmm)
        zc = P.op("vector", sDVE, lambda v: v.tensor_copy(out=zb[:, :],
                                                          in_=pf[4][:, 0:96]))
        P.wait("tensor", sDVE, zc)
        ztr = P.op("tensor", sPE, lambda t: t.transpose(
            pf[5][:, :].bitcast(BF16)[0:96, 0:128], zb[:, :], ident_b[:, :]))
        P.wait("vector", sPE, ztr)
        zc2 = P.op("vector", sDVE, lambda v: v.tensor_copy(
            out=zTt[:, :], in_=pf[5][:, :].bitcast(BF16)[0:96, 0:128]))
        zc2_prev[0] = zc2
        P.wait("tensor", sDVE, zc2)
        kdone = 0

        def emit_half(half):
            for ic in range(16):
                for j in range(4):
                    oc = half * 4 + j
                    P.opn("tensor", lambda t, ic=ic, j=j, oc=oc, lbuf=lbuf:
                          t.matmul(pf[j][:, :],
                                   lbuf[:, ic * 128:(ic + 1) * 128],
                                   wqkv_s[:, ic * 4096 + oc * 512:
                                          ic * 4096 + (oc + 1) * 512],
                                   start=(ic == 0), stop=False))
            hmm = 0
            for j in range(4):
                oc = half * 4 + j
                def mmfn(t, j=j, oc=oc):
                    return t.matmul(
                        pf[j][:, :], zTt[0:96, :],
                        bcomb_s[0:96, oc * 512:(oc + 1) * 512],
                        start=False, stop=True)
                if j == 3:
                    hmm = P.op("tensor", sPE, mmfn)
                else:
                    P.opn("tensor", mmfn)
            return hmm

        hmm0 = emit_half(0)
        P.wait("vector", sPE, hmm0)
        P.wait("vector", sACT)
        ec0 = 0
        for j in range(4):
            ec0 = P.op("vector", sDVE, lambda v, j=j: v.tensor_copy(
                out=qf4[:, j * 512:(j + 1) * 512], in_=pf[j][:, :]))
        P.wait("tensor", sDVE, ec0)
        hmm1 = emit_half(1)
        for oc in range(4):
            qdma_done = process_q(P, oc, tt, None)
        P.wait("vector", sPE, hmm1)
        P.wait("vector", sACT)
        P.wait("scalar", sPE, hmm1)
        for j in range(4):
            oc = 4 + j
            if oc < 6:
                ec = P.op("vector", sDVE, lambda v, j=j, oc=oc: v.tensor_copy(
                    out=qf4[:, (oc % 4) * 512:(oc % 4 + 1) * 512],
                    in_=pf[j][:, :]))
            else:
                P.wait("scalar", sG)
                ec = P.op("scalar", sACT, lambda s, j=j, oc=oc: s.activation(
                    out=vst[:, (oc - 6) * 512:(oc - 5) * 512],
                    in_=pf[j][:, :], func=AF.Copy))
                vst_prev[0] = ec
        for oc in (4, 5):
            kdone = process_q(P, oc, tt, None)
        P.wait("gpsimd", sDVE, kdone)
        P.wait("gpsimd", sACT)
        P.wait("gpsimd", sQSP)
        P.idma("gpsimd", kt_sh.ap(), IndirectOffsetOnAxis(kidx[:, tt:tt + 1], 0),
               ktst[:, :], None, sG)
        scat_done = P.idma("gpsimd", v_sh.ap(),
                           IndirectOffsetOnAxis(vidx[:, tt:tt + 1], 0),
                           vst[:, :], None, sG)
    P.wait("gpsimd", sG, scat_done)
    P.idma("gpsimd", flag_sh.ap(), IndirectOffsetOnAxis(fidx[:, :], 0),
           tokrow[:, :], None, sG)
    P.run()

    if stop == "qkv":
        return finisher()

    P = Ph(nc)
    mld = P.dma("sync", masks_s[:, :].rearrange("p (s c) -> p s c", c=512),
                masks_in.ap().rearrange("s p c -> p s c"), sONE)

    flagw = sG.c

    def spin(g):
        g.wait_ge(sG.h, flagw)
        tokA = nc.values_load(tokrow[0:1, 0:1], engines=[ET.Pool],
                              skip_runtime_bounds_check=True)
        tokB = nc.values_load(tokrow[1:2, 0:1], engines=[ET.Pool],
                              skip_runtime_bounds_check=True)
        nc.br("spin_loop", engines=[ET.Pool])
        with nc.body("spin_loop", valid_engines=[ET.Pool]):
            g.dma_start(out=flagv[:, :], in_=flag_sh.ap()).then_inc(sPoll.h, 16)
            g.nop(cycle_cnt=4000)
            v0 = nc.values_load(flagv[0:1, 0:1], engines=[ET.Pool],
                                skip_runtime_bounds_check=True)
            nc.br_cmp(v0, tokA, on_true="spin_c1", on_false="spin_loop",
                      comp_op="IS_EQ", engines=[ET.Pool])
        with nc.body("spin_c1", valid_engines=[ET.Pool]):
            v1 = nc.values_load(flagv[1:2, 0:1], engines=[ET.Pool],
                                skip_runtime_bounds_check=True)
            nc.br_cmp(v1, tokB, on_true="spin_c2", on_false="spin_loop",
                      comp_op="IS_EQ", engines=[ET.Pool])
        with nc.body("spin_c2", valid_engines=[ET.Pool]):
            v2 = nc.values_load(flagv[2:3, 0:1], engines=[ET.Pool],
                                skip_runtime_bounds_check=True)
            nc.br_cmp(v2, tokA, on_true="spin_c3", on_false="spin_loop",
                      comp_op="IS_EQ", engines=[ET.Pool])
        with nc.body("spin_c3", valid_engines=[ET.Pool]):
            v3 = nc.values_load(flagv[3:4, 0:1], engines=[ET.Pool],
                                skip_runtime_bounds_check=True)
            nc.br_cmp(v3, tokB, on_true="spin_done", on_false="spin_loop",
                      comp_op="IS_EQ", engines=[ET.Pool])
        nc.switch_bb("spin_done")
        g.nop().then_inc(sSpin.h, 1)
    if DEBUG_PRIVATE[0]:
        P.wait("gpsimd", sG)

        def spin2(g):
            g.nop().then_inc(sSpin.h, 1)
        P.L["gpsimd"].append(spin2)
        sSpin.c += 1
    else:
        P.L["gpsimd"].append(spin)
        sSpin.c += 1

    ktv = kt_sh.ap().rearrange("(dk g) (kvh tw) -> dk g kvh tw", g=32, tw=128)
    vtv = v_sh.ap().rearrange("(g p) (kvh dv) -> p g kvh dv", p=128, dv=128)

    seq = []
    for kvh in range(8):
        for qi in range(2):
            qh = 2 * kvh + qi
            for slot in range(16):
                seq.append(dict(kvh=kvh, qh=qh, slot=slot,
                                nch=NCH[slot], npad=NPAD[slot],
                                fk=(qi == 0 and slot == 0), fq=(slot == 0)))
    NS = len(seq)
    kl_tgt = [0] * 8
    vl_tgt = [0] * 8
    ql_tgt = [0] * 16

    def emit_kvh_load(kvh, war):
        if war is not None:
            P.wait("sync", sPE, war)
        kl_tgt[kvh] = P.dma(
            "sync", ktb[kvh % 2][:, :].rearrange("p (g t) -> p g t", t=128),
            ktv[:, :, kvh, :], sKT[kvh % 2])
        vl_tgt[kvh] = P.dma(
            "sync", vbuf[kvh % 2][:, :].rearrange("p (g d) -> p g d", d=128),
            vtv[:, :, kvh, :], sVB[kvh % 2])

    def emit_qh_load(qh, war):
        if war is not None:
            P.wait("sync", sPE, war)
        ql_tgt[qh] = P.dma(
            "sync", qtb[qh % 2][:, :].rearrange("p (s t) -> p s t", t=128),
            qtsp.ap()[qh, :, :, :].rearrange("s p t -> p s t"), sQT[qh % 2])

    P.wait("sync", sQSP)
    P.wait("vector", sONE)
    emit_qh_load(0, None)
    emit_qh_load(1, None)
    P.wait("sync", sSpin)
    emit_kvh_load(0, None)
    emit_kvh_load(1, None)

    s_cnt = [None] * NS
    evac_cnt = [None] * NS
    negmax_cnt = [0] * NS
    exp_cnt = [0] * NS
    recip_cnt = [0] * NS
    norm_cnt = [0] * NS
    t_end = [0] * NS
    av_end = [0] * NS
    y_cnt = [0] * NS
    ptc_bank = {6: 0, 7: 0}

    def emit_av_grp(j, g, ngrp, npad, kvh, avb, pt_sb, ptc, m):
        P.wait("tensor", sACT, ptc[g])
        if g == 0:
            if j >= 2:
                P.wait("tensor", sACT, y_cnt[j - 2])
            if m['fk']:
                P.wait("tensor", sVB[kvh % 2], vl_tgt[kvh])
        j0, j1 = g * 8, min(npad, g * 8 + 8)
        for jj in range(j0, j1):
            def mmfn(t, jj=jj, avb=avb, kvh=kvh, npad=npad, pt_sb=pt_sb):
                return t.matmul(pf[avb][:, 0:128],
                                vbuf[kvh % 2][:, jj * 128:(jj + 1) * 128],
                                pt_sb[:, jj * 128:(jj + 1) * 128],
                                start=(jj == 0), stop=(jj == npad - 1))
            if jj == npad - 1:
                av_end[j] = P.op("tensor", sPE, mmfn)
            else:
                P.opn("tensor", mmfn)

    for st in range(NS + 2):
        i = st if st < NS else None
        j = st - 2 if st >= 2 else None
        if st + 24 < NS and (st + 24) % 32 == 0:
            kvh_n = (st + 24) // 32
            if kvh_n >= 2:
                emit_kvh_load(kvh_n, av_end[st - 9])
        if st + 8 < NS and (st + 8) % 16 == 0:
            qh_n = (st + 8) // 16
            if qh_n >= 2:
                emit_qh_load(qh_n, s_cnt[st - 9][-1])
        if j is not None:
            m = seq[j]
            sp = j % 2
            cs_, cr = 25 + 4 * sp, 26 + 4 * sp
            P.wait("vector", sACT, exp_cnt[j])
            recip_cnt[j] = P.op("vector", sDVE, lambda v, cs_=cs_, cr=cr:
                                v.reciprocal(out=cols[:, cr:cr + 1],
                                             in_=cols[:, cs_:cs_ + 1]))
            npad, p_sb = m['npad'], p_sb2[sp]
            norm_cnt[j] = P.op("vector", sDVE, lambda v, npad=npad, p_sb=p_sb,
                               cr=cr: v.tensor_scalar(
                                   out=p_sb[:, 0:npad * 128],
                                   in0=p_sb[:, 0:npad * 128],
                                   scalar1=cols[:, cr:cr + 1], scalar2=None,
                                   op0=ALU.mult))
        e = st - 1
        if 0 <= e < NS:
            m = seq[e]
            sp = e % 2
            cm, cs_ = 24 + 4 * sp, 25 + 4 * sp
            nch = m['nch']
            s_sb, p_sb = s_sb2[sp], p_sb2[sp]
            P.wait("scalar", sDVE, negmax_cnt[e])
            if e - 2 >= 0:
                P.wait("scalar", sPE, t_end[e - 2])
                P.wait("scalar", sDVE, recip_cnt[e - 2])
            npad_e = m['npad']
            exp_cnt[e] = P.op("scalar", sACT, lambda q, npad_e=npad_e, s_sb=s_sb,
                              p_sb=p_sb, cm=cm, cs_=cs_: q.activation(
                                  out=p_sb[:, 0:npad_e * 128],
                                  in_=s_sb[:, 0:npad_e * 128],
                                  func=AF.Exp, bias=cols[:, cm:cm + 1],
                                  accum_out=cols[:, cs_:cs_ + 1]))
        if i is not None:
            m = seq[i]
            sp = i % 2
            cm = 24 + 4 * sp
            nch, slot, qh, kvh = m['nch'], m['slot'], m['qh'], m['kvh']
            s_sb = s_sb2[sp]
            base = 0 if sp == 0 else 4
            if m['fq']:
                P.wait("tensor", sQT[qh % 2], ql_tgt[qh])
            if m['fk']:
                P.wait("tensor", sKT[kvh % 2], kl_tgt[kvh])
            s_cnt[i] = []
            evac_cnt[i] = []
            for kc in range(nch):
                bank = base + (kc % 2)
                if kc >= 2:
                    P.wait("tensor", sDVE, evac_cnt[i][kc - 2])
                elif kc == 0 and i >= 2:
                    P.wait("tensor", sDVE, evac_cnt[i - 2][-1])
                s_cnt[i].append(P.op("tensor", sPE, lambda t, slot=slot, kc=kc,
                                     bank=bank, kvh=kvh, qh=qh: t.matmul(
                                         pf[bank][:, :],
                                         qtb[qh % 2][:, slot * 128:(slot + 1) * 128],
                                         ktb[kvh % 2][:, kc * 512:(kc + 1) * 512],
                                         start=True, stop=True)))
                if kc == 0 and i >= 2:
                    P.wait("vector", sACT, exp_cnt[i - 2])
                P.wait("vector", sPE, s_cnt[i][kc])
                if kc == nch - 1:
                    ev = P.op("vector", sDVE, lambda v, kc=kc, bank=bank,
                              s_sb=s_sb, slot=slot: v.tensor_tensor(
                                  out=s_sb[:, kc * 512:(kc + 1) * 512],
                                  in0=pf[bank][:, :],
                                  in1=masks_s[:, slot * 512:(slot + 1) * 512],
                                  op=ALU.add))
                else:
                    ev = P.op("vector", sDVE, lambda v, kc=kc, bank=bank,
                              s_sb=s_sb: v.tensor_copy(
                                  out=s_sb[:, kc * 512:(kc + 1) * 512],
                                  in_=pf[bank][:, :]))
                evac_cnt[i].append(ev)
            npad_i = m['npad']
            negmax_cnt[i] = P.op("vector", sDVE, lambda v, npad_i=npad_i,
                                 s_sb=s_sb, cm=cm: v.reduce_max(
                                     cols[:, cm:cm + 1], s_sb[:, 0:npad_i * 128],
                                     axis=AX.X, negate=True))
        if j is not None:
            m = seq[j]
            sp = j % 2
            npad, kvh, qh, slot = m['npad'], m['kvh'], m['qh'], m['slot']
            p_sb, pt_sb = p_sb2[sp], pt_sb2[sp]
            avb = 2 + sp
            ngrp = (npad + 7) // 8
            P.wait("tensor", sDVE, norm_cnt[j])
            ptc = [0] * ngrp
            for g in range(ngrp):
                pbank = 6 + (g % 2)
                pb = pf[pbank][:, :].bitcast(BF16)
                P.wait("tensor", sACT, ptc_bank[pbank])
                j0, j1 = g * 8, min(npad, g * 8 + 8)
                for jj in range(j0, j1):
                    def trfn(t, pb=pb, jj=jj, j0=j0, p_sb=p_sb):
                        return t.transpose(
                            pb[:, (jj - j0) * 128:(jj - j0 + 1) * 128],
                            p_sb[:, jj * 128:(jj + 1) * 128], ident_b[:, :])
                    if jj == j1 - 1:
                        tr = P.op("tensor", sPE, trfn)
                    else:
                        P.opn("tensor", trfn)
                P.wait("scalar", sPE, tr)
                if g == 0 and j >= 2:
                    P.wait("scalar", sPE, av_end[j - 2])
                ptc[g] = P.op("scalar", sACT, lambda q, pb=pb, j0=j0, j1=j1,
                              pt_sb=pt_sb: q.copy(
                                  out=pt_sb[:, j0 * 128:j1 * 128],
                                  in_=pb[:, 0:(j1 - j0) * 128]))
                ptc_bank[pbank] = ptc[g]
                if g >= 1:
                    emit_av_grp(j, g - 1, ngrp, npad, kvh, avb, pt_sb, ptc, m)
            emit_av_grp(j, ngrp - 1, ngrp, npad, kvh, avb, pt_sb, ptc, m)
            t_end[j] = tr
            P.wait("scalar", sPE, av_end[j])
            y_cnt[j] = P.op("scalar", sACT, lambda q, qh=qh, slot=slot, avb=avb:
                            q.activation(
                                out=yT_at[:, qh * 2048 + slot * 128:
                                          qh * 2048 + (slot + 1) * 128],
                                in_=pf[avb][:, 0:128], func=AF.Copy))
    P.run()

    if stop == "attn":
        return finisher()

    P = Ph(nc)
    wold = P.dma("gpsimd", wo_s[:, :].rearrange("p (ic o) -> p ic o", o=2048),
                 woT_in.ap().rearrange("(ic p) o -> p ic o", p=128), sONEG)
    aold = P.dma("gpsimd", aoT_s[:, :].rearrange("p (ic r) -> p ic r", r=32),
                 aoT_in.ap().rearrange("(ic p) r -> p ic r", p=128), sONEG)
    bold = P.dma("gpsimd", boT_s[:, :], boT_in.ap(), sONEG)
    P.wait("vector", sONEG)
    sofold = 0
    for ic in range(16):
        sofold = P.op("vector", sDVE, lambda v, ic=ic: v.tensor_scalar(
            out=wo_s[:, ic * 2048:(ic + 1) * 2048],
            in0=wo_s[:, ic * 2048:(ic + 1) * 2048],
            scalar1=sb_all[:, 3:4], scalar2=None, op0=ALU.mult))
    P.wait("tensor", sONEG)
    P.wait("tensor", sDVE, sofold)
    ln3_x2_done = 0
    x2dp = 0
    for tt in range(NTT):
        xb_ = x1rd[tt % 2]
        P.wait("scalar", sDVE)
        if tt == 0:
            P.wait("scalar", sSPILL)
        xl = P.dma("scalar", xb_[:, :], x1sp.ap()[tt * 128:(tt + 1) * 128, :],
                   sXR[tt % 2])
        for ic in range(16):
            def mmfn(t, ic=ic, tt=tt):
                return t.matmul(
                    pf[4][:, 0:32],
                    yT_at[:, ic * 2048 + tt * 128:ic * 2048 + (tt + 1) * 128],
                    aoT_s[:, ic * 32:(ic + 1) * 32],
                    start=(ic == 0), stop=(ic == 15))
            if ic == 15:
                zmm = P.op("tensor", sPE, mmfn)
            else:
                P.opn("tensor", mmfn)
        P.wait("vector", sPE, zmm)
        zc = P.op("vector", sDVE, lambda v: v.tensor_copy(out=zob[:, :],
                                                          in_=pf[4][:, 0:32]))

        def emit_main(ic0, ic1):
            for ic in range(ic0, ic1):
                for dc in range(4):
                    P.opn("tensor", lambda t, ic=ic, dc=dc, tt=tt: t.matmul(
                        pf[dc][:, :],
                        yT_at[:, ic * 2048 + tt * 128:ic * 2048 + (tt + 1) * 128],
                        wo_s[:, ic * 2048 + dc * 512:ic * 2048 + (dc + 1) * 512],
                        start=(ic == 0), stop=False))

        P.wait("tensor", sDVE, x2dp)
        emit_main(0, 8)
        P.wait("tensor", sDVE, zc)
        ztr = P.op("tensor", sPE, lambda t: t.transpose(
            pf[5][:, :].bitcast(BF16)[0:32, 0:128], zob[:, :], ident_b[:, :]))
        P.wait("vector", sPE, ztr)
        zc2 = P.op("vector", sDVE, lambda v: v.tensor_copy(
            out=zoT[:, :], in_=pf[5][:, :].bitcast(BF16)[0:32, 0:128]))
        emit_main(8, 16)
        P.wait("tensor", sDVE, zc2)
        for dc in range(4):
            def mmfn(t, dc=dc):
                return t.matmul(
                    pf[dc][:, :], zoT[0:32, :],
                    boT_s[0:32, dc * 512:(dc + 1) * 512],
                    start=False, stop=True)
            if dc == 3:
                wmm = P.op("tensor", sPE, mmfn)
            else:
                P.opn("tensor", mmfn)
        P.wait("vector", sPE, wmm)
        P.wait("vector", sXR[tt % 2], xl)
        P.wait("vector", sSPILL)
        P.wait("vector", sACT, ln3_x2_done)
        for dc in range(4):
            x2d = P.op("vector", sDVE, lambda v, dc=dc, xb_=xb_: v.tensor_tensor(
                out=x2t[:, dc * 512:(dc + 1) * 512], in0=pf[dc][:, :],
                in1=xb_[:, dc * 512:(dc + 1) * 512], op=ALU.add))
        x2dp = x2d
        P.wait("scalar", sDVE, x2d)
        P.dma("scalar", x2sp.ap()[tt * 128:(tt + 1) * 128, :], x2t[:, :], sSPILL)
        emit_ln_tile(P, x2t[:, :], lnb3, sqj3, lnx3T, tt)
        ln3_x2_done = sACT.c
    P.run()

    if stop == "wo":
        return finisher()

    P = Ph(nc)
    P.wait("sync", sSPILL)
    P.wait("gpsimd", sSPILL)
    P.wait("vector", sSPILL)
    P.wait("scalar", sSPILL)
    emit_ffn(P, lnx3T, x2sp, final=True, modes=("cast", "cast"))
    P.run()

    return nc


_NC_CACHE = {}


def _get_nc():
    if "nc" not in _NC_CACHE:
        _NC_CACHE["nc"] = build_nc()
    return _NC_CACHE["nc"]


def _host_inputs(inputs):
    x = np.asarray(inputs["L0_attn_x"], dtype=np.float32)
    Wq = np.asarray(inputs["L0_attn_W_q"], dtype=np.float32)
    Wk = np.asarray(inputs["L0_attn_W_k"], dtype=np.float32)
    Wv = np.asarray(inputs["L0_attn_W_v"], dtype=np.float32)
    Wo = np.asarray(inputs["L0_attn_W_o"], dtype=np.float32)
    Aq = np.asarray(inputs["L0_attn_A_q"], dtype=np.float32)
    Bq = np.asarray(inputs["L0_attn_B_q"], dtype=np.float32)
    Ak = np.asarray(inputs["L0_attn_A_k"], dtype=np.float32)
    Bk = np.asarray(inputs["L0_attn_B_k"], dtype=np.float32)
    Av = np.asarray(inputs["L0_attn_A_v"], dtype=np.float32)
    Bv = np.asarray(inputs["L0_attn_B_v"], dtype=np.float32)
    Ao = np.asarray(inputs["L0_attn_A_o"], dtype=np.float32)
    Bo = np.asarray(inputs["L0_attn_B_o"], dtype=np.float32)
    sq = float(np.asarray(inputs["L0_attn_s_q"]).reshape(-1)[0])
    sk = float(np.asarray(inputs["L0_attn_s_k"]).reshape(-1)[0])
    sv = float(np.asarray(inputs["L0_attn_s_v"]).reshape(-1)[0])
    so = float(np.asarray(inputs["L0_attn_s_o"]).reshape(-1)[0])
    qg = np.asarray(inputs["L0_attn_q_gain"], dtype=np.float32).reshape(16)
    wg = np.asarray(inputs["L0_ffn_w_gate"], dtype=np.float32)
    wu = np.asarray(inputs["L0_ffn_w_up"], dtype=np.float32)
    wd = np.asarray(inputs["L0_ffn_w_down"], dtype=np.float32)

    wgT = np.ascontiguousarray(
        wg.T.reshape(2048, 64, 128).transpose(1, 0, 2)).reshape(64 * 2048, 128)
    wuT = np.ascontiguousarray(
        wu.T.reshape(2048, 64, 128).transpose(1, 0, 2)).reshape(64 * 2048, 128)
    wdT = np.ascontiguousarray(wd.T)
    wqkvT = np.ascontiguousarray(np.concatenate([Wq.T, Wk.T, Wv.T], axis=1))
    woT = np.ascontiguousarray(Wo.T)
    acombT = np.ascontiguousarray(np.concatenate([Aq.T, Ak.T, Av.T], axis=1))
    bcombT = np.zeros((96, 2 * D), dtype=np.float32)
    bcombT[0:32, 0:2048] = Bq.T
    bcombT[32:64, 2048:3072] = Bk.T
    bcombT[64:96, 3072:4096] = Bv.T
    aoT = np.ascontiguousarray(Ao.T)
    boT = np.ascontiguousarray(Bo.T)
    svec = np.zeros((1, 32), dtype=np.float32)
    svec[0, 0:4] = [sq, sk, sv, so]
    svec[0, 4:20] = qg

    ident_b = np.eye(128, dtype=bfdt)
    ones_b = np.ones((1, 128), dtype=bfdt)

    j = np.arange(32, dtype=np.float64)
    inv_freq = 1.0 / (10000.0 ** (j / 32.0))
    pos = np.arange(4096, dtype=np.float64)
    ang = np.outer(pos, inv_freq)
    cosf = np.cos(ang).astype(np.float32)
    sinf = np.sin(ang).astype(np.float32)

    token = (int(time.time_ns()) & 0x7FFFFFF0) | 1
    tokA, tokB = token, token ^ 0x5A5A5A5A

    common = dict(wgT=wgT, wuT=wuT, wdT=wdT, wqkvT=wqkvT, woT=woT,
                  acombT=acombT, bcombT=bcombT, aoT=aoT, boT=boT, svec=svec,
                  ident_b=ident_b, ones_b=ones_b)

    in_maps = []
    for c in range(8):
        b, r = c // 2, c % 2
        G = ROLE_G[r]
        rows = np.concatenate([np.arange(g * 128, (g + 1) * 128) for g in G])
        xs = np.ascontiguousarray(x[b][rows])
        cos4 = np.ascontiguousarray(np.tile(cosf[rows], (1, 4)))
        sin4 = np.ascontiguousarray(np.tile(sinf[rows], (1, 4)))
        masks = np.zeros((16, 128, 512), dtype=np.float32)
        for s in range(16):
            g = G[s]
            kglob = (NCH[s] - 1) * 512 + np.arange(512)
            qglob = g * 128 + np.arange(128)
            masks[s] = np.where(kglob[None, :] <= qglob[:, None], 0.0, -1.0e9)
        masksb = masks.astype(bfdt)
        kidx = np.zeros((128, 16), dtype=np.uint32)
        vidx_ = np.zeros((128, 16), dtype=np.uint32)
        p = np.arange(128, dtype=np.uint32)
        for s in range(16):
            kidx[:, s] = p * 32 + G[s]
            vidx_[:, s] = G[s] * 128 + p
        fidx = (np.array([0, 1], dtype=np.uint32) + 2 * r).reshape(2, 1)
        tok = np.zeros((2, 64), dtype=np.uint32)
        tok[0, :] = tokA
        tok[1, :] = tokB
        in_maps.append(dict(common, x_in=xs, cos4=cos4, sin4=sin4, masks=masksb,
                            kidx=kidx, vidx=vidx_, fidx=fidx, tok=tok))
    return in_maps


def kernel(**inputs):
    in_maps = _host_inputs(inputs)
    nc = _get_nc()
    res = run_bass_kernel_spmd(nc, in_maps, core_ids=list(range(8)))
    out = np.zeros((4, 4096, 2048), dtype=np.float32)
    for c in range(8):
        b, r = c // 2, c % 2
        G = ROLE_G[r]
        shard = res.results[c]["y_out"]
        for s in range(16):
            out[b, G[s] * 128:(G[s] + 1) * 128, :] = shard[s * 128:(s + 1) * 128, :]
    return out


# version 14
# speedup vs baseline: 1.1540x; 1.1540x over previous
import math
import time
import numpy as np
import ml_dtypes
import concourse.bass as bass
import concourse.mybir as mybir
from concourse.bass_utils import run_bass_kernel_spmd
from concourse.bass_core_rust import IndirectOffsetOnAxis

F32 = mybir.dt.float32
BF16 = mybir.dt.bfloat16
U32 = mybir.dt.uint32
AF = mybir.ActivationFunctionType
ALU = mybir.AluOpType
AX = mybir.AxisListType
ET = mybir.EngineType

ROLE_G = [
    [0, 31, 2, 29, 4, 27, 6, 25, 8, 23, 10, 21, 12, 19, 14, 17],
    [1, 30, 3, 28, 5, 26, 7, 24, 9, 22, 11, 20, 13, 18, 15, 16],
]
NPAD = [s + 2 if s % 2 == 0 else 33 - s for s in range(16)]
NCH = [(n + 3) // 4 for n in NPAD]
REG0 = [NPAD[s] - 2 for s in range(16)]
WTIL = [4 * NCH[s] - NPAD[s] + 2 for s in range(16)]

D = 2048
T = 2048
NTT = 16
HID = 8192
RMS_EPS = 1.1920929e-07
LN_EPS = 1e-5
INV_SQRT_HD = 1.0 / math.sqrt(128.0)
bfdt = ml_dtypes.bfloat16


class Sem:
    def __init__(self, h):
        self.h = h
        self.c = 0


class Ph:

    def __init__(self, nc):
        self.nc = nc
        self.L = {"tensor": [], "vector": [], "scalar": [], "sync": [], "gpsimd": []}

    def dma(self, eng, out_ap, in_ap, sem, **kw):
        tgt = sem.c + 16

        def f(q, o=out_ap, i=in_ap, h=sem.h, kw=kw):
            q.dma_start(out=o, in_=i, **kw).then_inc(h, 16)
        self.L[eng].append(f)
        sem.c = tgt
        return tgt

    def idma(self, eng, out_ap, out_off, in_ap, in_off, sem):
        tgt = sem.c + 16

        def f(q, o=out_ap, oo=out_off, i=in_ap, io=in_off, h=sem.h):
            q.indirect_dma_start(out=o, out_offset=oo, in_=i,
                                 in_offset=io).then_inc(h, 16)
        self.L[eng].append(f)
        sem.c = tgt
        return tgt

    def wait(self, eng, sem, val=None):
        v = sem.c if val is None else val
        if v <= 0:
            return

        def f(q, h=sem.h, v=v):
            q.wait_ge(h, v)
        self.L[eng].append(f)

    def op(self, eng, sem, fn):
        if eng in ("vector", "scalar"):
            prev = sem.c

            def fw(q, h=sem.h, v=prev):
                if v > 0:
                    q.wait_ge(h, v)
            self.L[eng].append(fw)
        tgt = sem.c + 1

        def f(q, fn=fn, h=sem.h):
            fn(q).then_inc(h, 1)
        self.L[eng].append(f)
        sem.c = tgt
        return tgt

    def opn(self, eng, fn):
        self.L[eng].append(fn)

    def run(self):
        with self.nc.Block() as blk:
            blk.tensor(lambda e: [f(e) for f in self.L["tensor"]])
            blk.vector(lambda e: [f(e) for f in self.L["vector"]])
            blk.scalar(lambda e: [f(e) for f in self.L["scalar"]])
            blk.sync(lambda e: [f(e) for f in self.L["sync"]])
            blk.gpsimd(lambda e: [f(e) for f in self.L["gpsimd"]])


RACE_DETECT = [True]
DEBUG_PRIVATE = [False]


def build_nc(stop="full"):
    nc = bass.Bass("TRN2", target_bir_lowering=False,
                   detect_race_conditions=RACE_DETECT[0])

    def din(name, shape, dt=F32):
        return nc.dram_tensor(name, shape, dt, kind="ExternalInput")

    x_in = din("x_in", [T, D])
    wgT = din("wgT", [64 * D, 128])
    wuT = din("wuT", [64 * D, 128])
    wdT = din("wdT", [HID, D])
    wqkvT = din("wqkvT", [D, 2 * D])
    woT_in = din("woT", [D, D])
    acombT = din("acombT", [D, 96])
    bcombT = din("bcombT", [96, 2 * D])
    aoT_in = din("aoT", [D, 32])
    boT_in = din("boT", [32, D])
    svec_in = din("svec", [1, 32])
    ident_b_in = din("ident_b", [128, 128], BF16)
    cos4_in = din("cos4", [T, 128])
    sin4_in = din("sin4", [T, 128])
    masks_in = din("masks", [16, 128, 512], BF16)
    kidx_in = din("kidx", [128, 16], U32)
    vidx_in = din("vidx", [128, 16], U32)
    fidx_in = din("fidx", [2, 1], U32)
    tok_in = din("tok", [2, 64], U32)
    ones_b_in = din("ones_b", [1, 128], BF16)

    y_out = nc.dram_tensor("y_out", [T, D], F32, kind="ExternalOutput")

    lnx1T = nc.dram_tensor("lnx1T", [16, 128, T], BF16)
    lnx2T = nc.dram_tensor("lnx2T", [16, 128, T], BF16)
    lnx3T = nc.dram_tensor("lnx3T", [16, 128, T], BF16)
    x1sp = nc.dram_tensor("x1sp", [T, D], F32)
    x2sp = nc.dram_tensor("x2sp", [T, D], F32)
    qtsp = nc.dram_tensor("qtsp", [16, 16, 128, 128], BF16)

    _spc = "Local" if DEBUG_PRIVATE[0] else "Shared"
    kt_sh = nc.dram_tensor("kt_sh", [4096, 1024], BF16, addr_space=_spc)
    v_sh = nc.dram_tensor("v_sh", [4096, 1024], BF16, addr_space=_spc)
    flag_sh = nc.dram_tensor("flag_sh", [4, 64], U32, addr_space=_spc)
    wg_bf = nc.dram_tensor("wg_bf", [128, 131072], BF16)
    wu_bf = nc.dram_tensor("wu_bf", [128, 131072], BF16)
    wd_bf = nc.dram_tensor("wd_bf", [128, 131072], BF16)

    BASE = 0x4080
    _A = nc.alloc_sbuf_tensor_at
    def A(name, shape, dt, offset):
        return _A(name, shape, dt, offset=BASE + offset)
    ident_b = A("ident_b_s", [128, 128], BF16, offset=0x00000)
    sb_all = A("sb_all", [128, 32], F32, offset=0x00300)
    kidx = A("kidx_s", [128, 16], U32, offset=0x00380)
    vidx = A("vidx_s", [128, 16], U32, offset=0x003C0)
    cols = A("cols", [128, 64], F32, offset=0x00400)
    flagv = A("flagv", [4, 64], U32, offset=0x00500)
    tokrow = A("tokrow", [2, 64], U32, offset=0x00600)
    fidx = A("fidx_s", [2, 1], U32, offset=0x00700)
    svec_s = A("svec_s", [1, 32], F32, offset=0x00720)
    svec_b = A("svec_b", [1, 32], BF16, offset=0x007A0)
    ones_b = A("ones_b_s", [1, 128], BF16, offset=0x007E0)
    c_lneps = A("c_lneps", [128, 1], F32, offset=0x00900)
    c_rmseps = A("c_rmseps", [128, 1], F32, offset=0x00920)
    c_neghalf = A("c_neghalf", [128, 1], F32, offset=0x00940)
    c_inv128 = A("c_inv128", [128, 1], F32, offset=0x00960)
    lnstage = A("lnstage", [128, 16 * 128], BF16, offset=0x2EA00)

    hid_rot = A("hid_rot", [128, 8 * 1024], BF16, offset=0x00A00)
    yacc = A("yacc", [128, 8 * 2048], F32, offset=0x04A00)
    lnx_ffn = A("lnx_ffn", [128, 16 * 1024], BF16, offset=0x14A00)
    wd2 = [A("wd2a", [128, 8 * 1024], BF16, offset=0x1CA00),
           A("wd2b", [128, 8 * 1024], BF16, offset=0x20A00)]
    wgb = [A("wgba", [128, 16 * 128], BF16, offset=0x24A00),
           A("wgbb", [128, 16 * 128], BF16, offset=0x25A00)]
    wub = [A("wuba", [128, 16 * 128], BF16, offset=0x26A00),
           A("wubb", [128, 16 * 128], BF16, offset=0x27A00)]
    x0b = [A("x0ba", [128, 2048], F32, offset=0x28A00),
           A("x0bb", [128, 2048], F32, offset=0x2AA00)]
    lnb = A("lnb", [128, 2048], BF16, offset=0x2CA00)
    sqj = A("sqj", [128, 2048], BF16, offset=0x2DA00)

    wqkv_s = A("wqkv_s", [128, 16 * 4096], BF16, offset=0x00A00)
    cos4 = A("cos4_s", [128, 16 * 128], F32, offset=0x20A00)
    sin4 = A("sin4_s", [128, 16 * 128], F32, offset=0x22A00)
    acomb_s = A("acomb_s", [128, 16 * 96], BF16, offset=0x24A00)
    bcomb_s = A("bcomb_s", [96, 4096], BF16, offset=0x25800)
    lnq = [A("lnqa", [128, 16 * 128], BF16, offset=0x27800),
           A("lnqb", [128, 16 * 128], BF16, offset=0x28800)]
    qf4 = A("qf4", [128, 2048], F32, offset=0x29800)
    qrope = A("qrope", [128, 512], F32, offset=0x2B800)
    sqv = A("sqv", [128, 512], F32, offset=0x2C000)
    ktst = A("ktst", [128, 1024], BF16, offset=0x2C800)
    vst = A("vst", [128, 1024], BF16, offset=0x2D000)
    qtst = A("qtst", [128, 512], BF16, offset=0x2D800)
    qstage = A("qstage", [128, 512], BF16, offset=0x2DC00)
    zb = A("zb", [128, 96], BF16, offset=0x2E000)
    zTt = A("zTt", [96, 128], BF16, offset=0x2E200)

    yT_at = A("yT_at", [128, 16 * 2048], BF16, offset=0x00A00)
    masks_s = A("masks_s", [128, 16 * 512], BF16, offset=0x10A00)
    s_sb2 = [A("s_sba", [128, 4096], F32, offset=0x15A00),
             A("s_sbb", [128, 4096], F32, offset=0x19A00)]
    p_sb2 = [A("p_sba", [128, 4096], BF16, offset=0x1DA00),
             A("p_sbb", [128, 4096], BF16, offset=0x1FA00)]
    pt_sb2 = [A("pt_sba", [128, 32 * 128], BF16, offset=0x21A00),
              A("pt_sbb", [128, 32 * 128], BF16, offset=0x23A00)]
    ktb = [A("ktba", [128, 4096], BF16, offset=0x25A00),
           A("ktbb", [128, 4096], BF16, offset=0x27A00)]
    vbuf = [A("vba", [128, 4096], BF16, offset=0x29A00),
            A("vbb", [128, 4096], BF16, offset=0x2BA00)]
    qtb = [A("qtba", [128, 16 * 128], BF16, offset=0x2DA00),
           A("qtbb", [128, 16 * 128], BF16, offset=0x2EA00)]

    wo_s = A("wo_s", [128, 16 * 2048], BF16, offset=0x10A00)
    aoT_s = A("aoT_s", [128, 16 * 32], BF16, offset=0x20A00)
    boT_s = A("boT_s", [32, 2048], BF16, offset=0x20E00)
    x1rd = [A("x1rda", [128, 2048], F32, offset=0x21E00),
            A("x1rdb", [128, 2048], F32, offset=0x23E00)]
    x2t = A("x2t", [128, 2048], F32, offset=0x25E00)
    zob = A("zob", [128, 32], BF16, offset=0x27E00)
    zoT = A("zoT", [32, 128], BF16, offset=0x27F00)
    lnb3 = A("lnb3", [128, 2048], BF16, offset=0x28000)
    sqj3 = A("sqj3", [128, 2048], BF16, offset=0x29000)

    pf = [nc.alloc_psum_tensor(f"pf{i}", [128, 512], F32) for i in range(8)]

    sW = Sem(nc.alloc_semaphore("sW"))
    sX = Sem(nc.alloc_semaphore("sX"))
    sG = Sem(nc.alloc_semaphore("sG"))
    sONE = Sem(nc.alloc_semaphore("sONE"))
    sONEG = Sem(nc.alloc_semaphore("sONEG"))
    sLNS = Sem(nc.alloc_semaphore("sLNS"))
    sLNX = Sem(nc.alloc_semaphore("sLNX"))
    sSPILL = Sem(nc.alloc_semaphore("sSPILL"))
    sQSP = Sem(nc.alloc_semaphore("sQSP"))
    sX0 = [Sem(nc.alloc_semaphore("sX0a")), Sem(nc.alloc_semaphore("sX0b"))]
    sXR = [Sem(nc.alloc_semaphore("sXRa")), Sem(nc.alloc_semaphore("sXRb"))]
    sWG = [Sem(nc.alloc_semaphore("sWGa")), Sem(nc.alloc_semaphore("sWGb"))]
    sWD = [Sem(nc.alloc_semaphore("sWDa")), Sem(nc.alloc_semaphore("sWDb"))]
    sLQ = [Sem(nc.alloc_semaphore("sLQa")), Sem(nc.alloc_semaphore("sLQb"))]
    sKT = [Sem(nc.alloc_semaphore("sKTa")), Sem(nc.alloc_semaphore("sKTb"))]
    sVB = [Sem(nc.alloc_semaphore("sVBa")), Sem(nc.alloc_semaphore("sVBb"))]
    sQT = [Sem(nc.alloc_semaphore("sQTa")), Sem(nc.alloc_semaphore("sQTb"))]
    sPE = Sem(nc.alloc_semaphore("sPE"))
    sDVE = Sem(nc.alloc_semaphore("sDVE"))
    sACT = Sem(nc.alloc_semaphore("sACT"))
    sSpin = Sem(nc.alloc_semaphore("sSpin"))
    sPoll = Sem(nc.alloc_semaphore("sPoll"))
    sOut = Sem(nc.alloc_semaphore("sOut"))
    sPL = Sem(nc.alloc_semaphore("sPL"))
    sWGS = Sem(nc.alloc_semaphore("sWGS"))
    sWDS = Sem(nc.alloc_semaphore("sWDS"))

    P = Ph(nc)
    P.dma("sync", ident_b[:, :], ident_b_in.ap(), sONE)
    P.dma("sync", kidx[:, :], kidx_in.ap(), sONE)
    P.dma("sync", vidx[:, :], vidx_in.ap(), sONE)
    P.dma("sync", svec_s[:, :], svec_in.ap(), sONE)
    P.dma("sync", tokrow[:, :], tok_in.ap(), sONE)
    P.dma("sync", fidx[:, :], fidx_in.ap(), sONE)
    P.dma("sync", ones_b[:, :], ones_b_in.ap(), sONE)
    P.wait("vector", sONE)
    P.op("vector", sDVE, lambda v: v.memset(flagv[:, :], 0))
    P.op("vector", sDVE, lambda v: v.memset(c_lneps[:, :], LN_EPS))
    P.op("vector", sDVE, lambda v: v.memset(c_rmseps[:, :], RMS_EPS))
    P.op("vector", sDVE, lambda v: v.memset(c_neghalf[:, :], -0.5))
    P.op("vector", sDVE, lambda v: v.memset(c_inv128[:, :], 1.0 / 128.0))
    P.op("vector", sDVE, lambda v: v.tensor_copy(out=svec_b[:, :], in_=svec_s[:, :]))
    P.wait("tensor", sDVE)
    P.op("tensor", sPE, lambda t: t.matmul(pf[7][0:128, 0:32], ones_b[0:1, :],
                                           svec_b[0:1, :], start=True, stop=True))
    P.wait("vector", sPE)
    P.op("vector", sDVE, lambda v: v.tensor_copy(out=sb_all[:, :],
                                                 in_=pf[7][0:128, 0:32]))
    P.op("vector", sDVE, lambda v: v.tensor_scalar(
        out=sb_all[:, 4:20], in0=sb_all[:, 4:20], scalar1=INV_SQRT_HD,
        scalar2=None, op0=ALU.mult))
    P.run()

    def finisher():
        P = Ph(nc)
        for tt in range(NTT):
            P.dma("sync", y_out.ap()[tt * 128:(tt + 1) * 128, :],
                  x_in.ap()[tt * 128:(tt + 1) * 128, :], sOut)
        P.wait("sync", sOut)
        P.run()
        return nc

    if stop == "p0":
        return finisher()

    TRC = [0]
    LNSTORE = [0]

    def emit_ln_tile(P, xtile, lnbuf, sqbuf, scratch, tt):
        cm, cs, cv = 0, 1, 2
        m1 = P.op("vector", sDVE,
                  lambda v: v.reduce_sum(cols[:, cm:cm + 1], xtile, axis=AX.X))
        P.wait("scalar", sDVE, m1)
        a1 = P.op("scalar", sACT, lambda s: s.activation(
            out=sqbuf[:, :], in_=xtile, func=AF.Square,
            accum_out=cols[:, cs:cs + 1]))
        P.wait("vector", sACT, a1)
        P.op("vector", sDVE, lambda v: v.tensor_scalar(
            out=cols[:, cm:cm + 1], in0=cols[:, cm:cm + 1], scalar1=1.0 / D,
            scalar2=None, op0=ALU.mult))
        P.op("vector", sDVE, lambda v: v.tensor_scalar(
            out=cols[:, cs:cs + 1], in0=cols[:, cs:cs + 1], scalar1=1.0 / D,
            scalar2=None, op0=ALU.mult))
        P.op("vector", sDVE, lambda v: v.tensor_tensor(
            out=cols[:, cv:cv + 1], in0=cols[:, cm:cm + 1],
            in1=cols[:, cm:cm + 1], op=ALU.mult))
        v1 = P.op("vector", sDVE, lambda v: v.tensor_tensor(
            out=cols[:, cv:cv + 1], in0=cols[:, cs:cs + 1],
            in1=cols[:, cv:cv + 1], op=ALU.subtract))
        P.wait("scalar", sDVE, v1)
        P.op("scalar", sACT, lambda s: s.activation(
            out=cols[:, cv:cv + 1], in_=cols[:, cv:cv + 1], func=AF.Ln,
            bias=c_lneps[:, :]))
        a2 = P.op("scalar", sACT, lambda s: s.activation(
            out=cols[:, cv:cv + 1], in_=cols[:, cv:cv + 1], func=AF.Exp,
            scale=c_neghalf[:, :]))
        P.wait("vector", sACT, a2)
        lnx_done = P.op("vector", sDVE, lambda v: v.tensor_scalar(
            out=lnbuf[:, :], in0=xtile, scalar1=cols[:, cm:cm + 1],
            scalar2=cols[:, cv:cv + 1], op0=ALU.subtract, op1=ALU.mult))
        P.wait("vector", sLNS, LNSTORE[0])
        for half in range(2):
            bank = pf[6 + (TRC[0] % 2)]
            TRC[0] += 1
            pb = bank[:, :].bitcast(BF16)
            P.wait("tensor", sDVE, lnx_done)
            for j in range(8):
                ic = half * 8 + j
                if j < 7:
                    P.opn("tensor", lambda t, pb=pb, j=j, ic=ic: t.transpose(
                        pb[:, j * 128:(j + 1) * 128],
                        lnbuf[:, ic * 128:(ic + 1) * 128], ident_b[:, :]))
                else:
                    tr = P.op("tensor", sPE, lambda t, pb=pb, j=j, ic=ic: t.transpose(
                        pb[:, j * 128:(j + 1) * 128],
                        lnbuf[:, ic * 128:(ic + 1) * 128], ident_b[:, :]))
            P.wait("vector", sPE, tr)
            P.op("vector", sDVE, lambda v, pb=pb, half=half: v.tensor_copy(
                out=lnstage[:, half * 1024:(half + 1) * 1024], in_=pb[:, :]))
        P.wait("sync", sDVE)
        dst = scratch.ap()[:, :, tt * 128:(tt + 1) * 128].rearrange("ic p t -> p ic t")
        LNSTORE[0] = P.dma("sync", dst,
                           lnstage[:, :].rearrange("p (ic t) -> p ic t", t=128), sLNS)

    P = Ph(nc)
    for tt in range(NTT):
        buf = x0b[tt % 2]
        P.wait("scalar", sDVE)
        ld = P.dma("scalar", buf[:, :], x_in.ap()[tt * 128:(tt + 1) * 128, :],
                   sX0[tt % 2])
        P.wait("vector", sX0[tt % 2], ld)
        P.wait("scalar", sX0[tt % 2], ld)
        emit_ln_tile(P, buf[:, :], lnb, sqj, lnx1T, tt)

    if stop == "ln1":
        P.run()
        return finisher()

    def emit_ffn(P, lnsrc, resid, final, max_tc=2, max_hb=8, do_down=True, do_tail=True, do_mm=True, do_act=True, modes=("cast", "cast")):
        wg_last = [0, 0]
        evac_last = [0, 0]
        evac_any = [0]
        down_mm_end = 0
        down_evac_end = 0
        evac_done_hb = 0
        wgstore = [0, 0]
        wdstore = [0, 0]
        for tc in range(max_tc):
            mode = modes[tc] if tc < len(modes) else "read"
            P.wait("sync", sPE)
            P.wait("sync", sLNS)
            lnld = P.dma("sync",
                         lnx_ffn[:, :].rearrange("p (ic t) -> p ic t", t=1024),
                         lnsrc.ap()[:, :, tc * 1024:(tc + 1) * 1024]
                         .rearrange("ic p t -> p ic t"), sLNX)
            for hb in range(max_hb):
                for otl in range(8):
                    ot = hb * 8 + otl
                    gb_, ub_ = wgb[ot % 2], wub[ot % 2]
                    if mode in ("write", "cast"):
                        P.wait("gpsimd", sPE, wg_last[ot % 2])
                        if mode == "write":
                            P.wait("gpsimd", sWGS, wgstore[ot % 2])
                        lw = P.dma("gpsimd",
                                   gb_[:, :].rearrange("p (ic o) -> p ic o", o=128),
                                   wgT.ap()[ot * 2048:(ot + 1) * 2048, :]
                                   .rearrange("(ic p) o -> p ic o", p=128),
                                   sWG[ot % 2])
                        lw2 = P.dma("gpsimd",
                                    ub_[:, :].rearrange("p (ic o) -> p ic o", o=128),
                                    wuT.ap()[ot * 2048:(ot + 1) * 2048, :]
                                    .rearrange("(ic p) o -> p ic o", p=128),
                                    sWG[ot % 2])
                        if mode == "write":
                            P.wait("sync", sWG[ot % 2], lw2)
                            P.dma("sync", wg_bf.ap()[:, ot * 2048:(ot + 1) * 2048],
                                  gb_[:, :], sWGS)
                            P.wait("scalar", sWG[ot % 2], lw2)
                            wgstore[ot % 2] = P.dma(
                                "scalar", wu_bf.ap()[:, ot * 2048:(ot + 1) * 2048],
                                ub_[:, :], sWGS)
                    else:
                        P.wait("sync", sPE, wg_last[ot % 2])
                        P.wait("scalar", sPE, wg_last[ot % 2])
                        lw = P.dma("sync", gb_[:, :],
                                   wg_bf.ap()[:, ot * 2048:(ot + 1) * 2048],
                                   sWG[ot % 2])
                        lw2 = P.dma("scalar", ub_[:, :],
                                    wu_bf.ap()[:, ot * 2048:(ot + 1) * 2048],
                                    sWG[ot % 2])
                    P.wait("tensor", sWG[ot % 2], lw2)
                    if ot == 0:
                        P.wait("tensor", sLNX, lnld)
                        P.wait("tensor", sDVE)
                    if otl == 0 and down_evac_end > 0:
                        P.wait("tensor", sDVE, down_evac_end)
                    P.wait("tensor", sDVE, evac_last[ot % 2])
                    b0 = (ot % 2) * 4
                    ga, gb2, ua, ub2 = pf[b0], pf[b0 + 1], pf[b0 + 2], pf[b0 + 3]
                    if not do_mm:
                        continue
                    for ic in range(16):
                        for ts in range(2):
                            P.opn("tensor", lambda t,
                                  ic=ic, ts=ts, gb_=gb_, ga=ga, gb2=gb2: t.matmul(
                                      (ga if ts == 0 else gb2)[:, :],
                                      gb_[:, ic * 128:(ic + 1) * 128],
                                      lnx_ffn[:, ic * 1024 + ts * 512:
                                              ic * 1024 + (ts + 1) * 512],
                                      start=(ic == 0), stop=(ic == 15)))
                        for ts in range(2):
                            def mmfn(t, ic=ic, ts=ts, ub_=ub_, ua=ua, ub2=ub2):
                                return t.matmul(
                                    (ua if ts == 0 else ub2)[:, :],
                                    ub_[:, ic * 128:(ic + 1) * 128],
                                    lnx_ffn[:, ic * 1024 + ts * 512:
                                            ic * 1024 + (ts + 1) * 512],
                                    start=(ic == 0), stop=(ic == 15))
                            if ic == 15 and ts == 1:
                                wg_last[ot % 2] = P.op("tensor", sPE, mmfn)
                            else:
                                P.opn("tensor", mmfn)
                    if not do_act:
                        continue
                    P.wait("scalar", sPE, wg_last[ot % 2])
                    P.wait("scalar", sDVE, evac_any[0])
                    for ts in range(2):
                        P.op("scalar", sACT, lambda s, ts=ts, ga=ga, gb2=gb2:
                             s.activation(out=lnb[:, ts * 512:(ts + 1) * 512],
                                          in_=(ga if ts == 0 else gb2)[:, :],
                                          func=AF.Silu))
                    P.wait("vector", sACT)
                    if down_mm_end > 0 and otl == 0:
                        P.wait("vector", sPE, down_mm_end)
                    for ts in range(2):
                        ev = P.op("vector", sDVE, lambda v, ts=ts, otl=otl, ua=ua,
                                  ub2=ub2: v.tensor_tensor(
                                      out=hid_rot[:, otl * 1024 + ts * 512:
                                                  otl * 1024 + (ts + 1) * 512],
                                      in0=lnb[:, ts * 512:(ts + 1) * 512],
                                      in1=(ua if ts == 0 else ub2)[:, :],
                                      op=ALU.mult))
                    evac_last[ot % 2] = ev
                    evac_any[0] = ev
                    evac_done_hb = ev
                if not do_down:
                    continue
                wdld = [0, 0]
                for half in range(2):
                    qn = "sync" if half == 0 else "scalar"
                    if mode in ("write", "cast"):
                        P.wait("gpsimd", sPE, down_mm_end)
                        if mode == "write":
                            P.wait("gpsimd", sWDS, wdstore[half])
                        wdld[half] = P.dma(
                            "gpsimd",
                            wd2[half][:, :].rearrange("p (hc d) -> p hc d", d=2048),
                            wdT.ap()[hb * 1024 + half * 512:
                                     hb * 1024 + (half + 1) * 512, :]
                            .rearrange("(hc p) d -> p hc d", p=128), sWD[half])
                        if mode == "write":
                            P.wait(qn, sWD[half], wdld[half])
                            wdstore[half] = P.dma(
                                qn, wd_bf.ap()[:, hb * 16384 + half * 8192:
                                               hb * 16384 + (half + 1) * 8192],
                                wd2[half][:, :], sWDS)
                    else:
                        P.wait(qn, sPE, down_mm_end)
                        wdld[half] = P.dma(
                            qn, wd2[half][:, :],
                            wd_bf.ap()[:, hb * 16384 + half * 8192:
                                       hb * 16384 + (half + 1) * 8192], sWD[half])
                P.wait("tensor", sWD[0], wdld[0])
                P.wait("tensor", sWD[1], wdld[1])
                P.wait("tensor", sDVE, evac_done_hb)
                if down_evac_end > 0:
                    P.wait("tensor", sDVE, down_evac_end)
                down_evac_par = getattr(emit_ffn, "_dep", None)
                if down_evac_par is None:
                    down_evac_par = [0, 0]
                    emit_ffn._dep = down_evac_par
                for tt in range(8):
                    b0 = (tt % 2) * 4
                    P.wait("tensor", sDVE, down_evac_par[tt % 2])
                    mme = None
                    for hc in range(8):
                        wdb = wd2[hc // 4]
                        hcl = hc % 4
                        for dc in range(4):
                            def mmfn(t, tt=tt, hc=hc, hcl=hcl, wdb=wdb,
                                     dc=dc, b0=b0):
                                return t.matmul(
                                    pf[b0 + dc][:, :],
                                    hid_rot[:, hc * 1024 + tt * 128:
                                            hc * 1024 + (tt + 1) * 128],
                                    wdb[:, hcl * 2048 + dc * 512:
                                        hcl * 2048 + (dc + 1) * 512],
                                    start=(hc == 0), stop=(hc == 7))
                            if hc == 7 and dc == 3:
                                mme = P.op("tensor", sPE, mmfn)
                            else:
                                P.opn("tensor", mmfn)
                    P.wait("vector", sPE, mme)
                    for dc in range(4):
                        if hb == 0:
                            de = P.op("vector", sDVE, lambda v, tt=tt, dc=dc,
                                      b0=b0: v.tensor_copy(
                                          out=yacc[:, tt * 2048 + dc * 512:
                                                   tt * 2048 + (dc + 1) * 512],
                                          in_=pf[b0 + dc][:, :]))
                        else:
                            de = P.op("vector", sDVE, lambda v, tt=tt, dc=dc,
                                      b0=b0: v.tensor_tensor(
                                          out=yacc[:, tt * 2048 + dc * 512:
                                                   tt * 2048 + (dc + 1) * 512],
                                          in0=pf[b0 + dc][:, :],
                                          in1=yacc[:, tt * 2048 + dc * 512:
                                                   tt * 2048 + (dc + 1) * 512],
                                          op=ALU.add))
                    down_evac_end = de
                    down_evac_par[tt % 2] = de
                down_mm_end = mme
            if not do_tail:
                continue
            P.wait("vector", sPE, down_mm_end)
            for tt in range(8):
                ttg = tc * 8 + tt
                buf = x0b[tt % 2]
                P.wait("scalar", sDVE)
                P.wait("scalar", sSPILL)
                if final:
                    P.wait("scalar", sOut)
                ld = P.dma("scalar", buf[:, :],
                           resid.ap()[ttg * 128:(ttg + 1) * 128, :], sX0[tt % 2])
                P.wait("vector", sX0[tt % 2], ld)
                P.wait("vector", sDVE, down_evac_end)
                x1done = P.op("vector", sDVE, lambda v, tt=tt, buf=buf:
                              v.scalar_tensor_tensor(
                                  out=buf[:, :],
                                  in0=yacc[:, tt * 2048:(tt + 1) * 2048],
                                  scalar=0.5, in1=buf[:, :],
                                  op0=ALU.mult, op1=ALU.add))
                if final:
                    P.wait("scalar", sDVE, x1done)
                    P.dma("scalar", y_out.ap()[ttg * 128:(ttg + 1) * 128, :],
                          buf[:, :], sOut)
                else:
                    P.wait("scalar", sDVE, x1done)
                    P.dma("scalar", x1sp.ap()[ttg * 128:(ttg + 1) * 128, :],
                          buf[:, :], sSPILL)
                    emit_ln_tile(P, buf[:, :], lnb, sqj, lnx2T, ttg)
        if final:
            P.wait("scalar", sOut)

    if stop == "ffnload":
        emit_ffn(P, lnx1T, x_in, final=False, max_tc=1, max_hb=1,
                 do_down=False, do_tail=False, do_mm=False)
        P.run()
        return finisher()
    if stop == "ffnmm":
        emit_ffn(P, lnx1T, x_in, final=False, max_tc=1, max_hb=1,
                 do_down=False, do_tail=False, do_act=False)
        P.run()
        return finisher()
    if stop == "ffngate":
        emit_ffn(P, lnx1T, x_in, final=False, max_tc=1, max_hb=1,
                 do_down=False, do_tail=False)
        P.run()
        return finisher()
    if stop == "ffndown":
        emit_ffn(P, lnx1T, x_in, final=False, max_tc=1, max_hb=1, do_tail=False)
        P.run()
        return finisher()
    if stop == "ffntc0":
        emit_ffn(P, lnx1T, x_in, final=True, max_tc=1)
        P.run()
        return nc
    emit_ffn(P, lnx1T, x_in, final=(stop == "ffn1"))
    P.run()

    if stop == "ffn1":
        return nc

    P = Ph(nc)
    P.wait("sync", sSPILL)
    P.wait("gpsimd", sSPILL)
    P.wait("vector", sSPILL)
    P.wait("scalar", sSPILL)
    wq_tgts = []
    for g in range(4):
        wq_tgts.append(P.dma(
            "gpsimd",
            wqkv_s[:, g * 16384:(g + 1) * 16384]
            .rearrange("p (ic o) -> p ic o", o=4096),
            wqkvT.ap()[g * 512:(g + 1) * 512, :]
            .rearrange("(ic p) o -> p ic o", p=128), sONEG))
    acld = P.dma("gpsimd", acomb_s[:, :].rearrange("p (ic r) -> p ic r", r=96),
                 acombT.ap().rearrange("(ic p) r -> p ic r", p=128), sONEG)
    bcld = P.dma("gpsimd", bcomb_s[:, :], bcombT.ap(), sONEG)
    c4 = P.dma("sync", cos4[:, :].rearrange("p (tt c) -> p tt c", c=128),
               cos4_in.ap().rearrange("(tt p) c -> p tt c", p=128), sONE)
    s4 = P.dma("sync", sin4[:, :].rearrange("p (tt c) -> p tt c", c=128),
               sin4_in.ap().rearrange("(tt p) c -> p tt c", p=128), sONE)
    fold_done = 0
    for g in range(4):
        P.wait("vector", sONEG, wq_tgts[g])
        for ic in range(g * 4, (g + 1) * 4):
            for oc in range(8):
                sj = 0 if oc < 4 else (1 if oc < 6 else 2)
                fold_done = P.op("vector", sDVE, lambda v, ic=ic, oc=oc, sj=sj:
                                 v.tensor_scalar(
                                     out=wqkv_s[:, ic * 4096 + oc * 512:
                                                ic * 4096 + (oc + 1) * 512],
                                     in0=wqkv_s[:, ic * 4096 + oc * 512:
                                                ic * 4096 + (oc + 1) * 512],
                                     scalar1=sb_all[:, sj:sj + 1], scalar2=None,
                                     op0=ALU.mult))
    P.wait("tensor", sONE)
    P.wait("tensor", sONEG)
    P.wait("tensor", sDVE, fold_done)

    scat_done = 0
    qdma_done = 0
    zc2_prev = [0]
    vst_prev = [0]
    trlast = [0]

    def process_q(P, oc, tt, slot4):
        src = qf4[:, (oc % 4) * 512:(oc % 4 + 1) * 512]
        P.wait("vector", sPE, trlast[0])
        P.wait("scalar", sDVE)
        sq1 = P.op("scalar", sACT, lambda s: s.activation(
            out=sqv[:, :], in_=src, func=AF.Square))
        P.wait("vector", sACT, sq1)
        P.op("vector", sDVE, lambda v: v.reduce_sum(
            cols[:, 8:12].rearrange("p (h o) -> p h o", o=1),
            sqv[:, :].rearrange("p (h c) -> p h c", c=128), axis=AX.X))
        P.wait("scalar", sDVE)
        P.op("scalar", sACT, lambda s: s.activation(
            out=cols[:, 8:12], in_=cols[:, 8:12], func=AF.Ln,
            scale=c_inv128[:, :], bias=c_rmseps[:, :]))
        rs = P.op("scalar", sACT, lambda s: s.activation(
            out=cols[:, 8:12], in_=cols[:, 8:12], func=AF.Exp,
            scale=c_neghalf[:, :]))
        P.wait("vector", sACT, rs)
        if oc < 4:
            P.op("vector", sDVE, lambda v: v.tensor_tensor(
                out=cols[:, 8:12], in0=cols[:, 8:12],
                in1=sb_all[:, 4 + 4 * oc:8 + 4 * oc], op=ALU.mult))
        xv = src.rearrange("p (h c) -> p h c", c=128)
        rv = qrope[:, :].rearrange("p (h c) -> p h c", c=128)
        sv = sqv[:, :].rearrange("p (h c) -> p h c", c=128)
        cosv = cos4[:, tt * 128:(tt + 1) * 128].rearrange("p (h c) -> p h c", c=32)
        sinv = sin4[:, tt * 128:(tt + 1) * 128].rearrange("p (h c) -> p h c", c=32)
        P.op("vector", sDVE, lambda v: v.tensor_tensor(
            out=rv[:, :, 0:32], in0=xv[:, :, 0:32], in1=cosv, op=ALU.mult))
        P.op("vector", sDVE, lambda v: v.tensor_tensor(
            out=sv[:, :, 0:32], in0=xv[:, :, 32:64], in1=sinv, op=ALU.mult))
        P.op("vector", sDVE, lambda v: v.tensor_tensor(
            out=rv[:, :, 0:32], in0=rv[:, :, 0:32], in1=sv[:, :, 0:32],
            op=ALU.add))
        P.op("vector", sDVE, lambda v: v.tensor_tensor(
            out=rv[:, :, 32:64], in0=xv[:, :, 32:64], in1=cosv, op=ALU.mult))
        P.op("vector", sDVE, lambda v: v.tensor_tensor(
            out=sv[:, :, 32:64], in0=xv[:, :, 0:32], in1=sinv, op=ALU.mult))
        P.op("vector", sDVE, lambda v: v.tensor_tensor(
            out=rv[:, :, 32:64], in0=rv[:, :, 32:64], in1=sv[:, :, 32:64],
            op=ALU.subtract))
        P.op("vector", sDVE, lambda v: v.tensor_copy(
            out=rv[:, :, 64:128], in_=xv[:, :, 64:128]))
        dstb = qtst if oc < 4 else None
        for h in range(4):
            P.op("vector", sDVE, lambda v, h=h: v.tensor_scalar(
                out=qtst[:, h * 128:(h + 1) * 128],
                in0=qrope[:, h * 128:(h + 1) * 128],
                scalar1=cols[:, 8 + h:9 + h], scalar2=None, op0=ALU.mult))
        tsdone = sDVE.c
        pb = pf[6][:, :].bitcast(BF16)
        P.wait("tensor", sDVE, tsdone)
        for h in range(4):
            tr = P.op("tensor", sPE, lambda t, h=h: t.transpose(
                pb[:, h * 128:(h + 1) * 128], qtst[:, h * 128:(h + 1) * 128],
                ident_b[:, :]))
        trlast[0] = tr
        P.wait("vector", sPE, tr)
        if oc < 4:
            P.wait("vector", sQSP)
            cp = P.op("vector", sDVE, lambda v: v.tensor_copy(
                out=qstage[:, :], in_=pb[:, 0:512]))
            P.wait("sync", sDVE, cp)
            return P.dma("sync",
                         qtsp.ap()[4 * oc:4 * oc + 4, tt, :, :]
                         .rearrange("h p t -> p h t"),
                         qstage[:, :].rearrange("p (h t) -> p h t", t=128), sQSP)
        else:
            koff = (oc - 4) * 512
            P.wait("vector", sG)
            return P.op("vector", sDVE, lambda v, koff=koff: v.tensor_copy(
                out=ktst[:, koff:koff + 512], in_=pb[:, 0:512]))

    P.wait("sync", sLNS)
    for tt in range(NTT):
        lbuf = lnq[tt % 2]
        P.wait("sync", sPE)
        lnld = P.dma("sync", lbuf[:, :].rearrange("p (ic t) -> p ic t", t=128),
                     lnx2T.ap()[:, :, tt * 128:(tt + 1) * 128]
                     .rearrange("ic p t -> p ic t"), sLQ[tt % 2])
        P.wait("tensor", sLQ[tt % 2], lnld)
        P.wait("tensor", sDVE, zc2_prev[0])
        P.wait("tensor", sACT, vst_prev[0])
        for ic in range(16):
            def mmfn(t, ic=ic, lbuf=lbuf):
                return t.matmul(
                    pf[4][:, 0:96], lbuf[:, ic * 128:(ic + 1) * 128],
                    acomb_s[:, ic * 96:(ic + 1) * 96],
                    start=(ic == 0), stop=(ic == 15))
            if ic == 15:
                zmm = P.op("tensor", sPE, mmfn)
            else:
                P.opn("tensor", mmfn)
        P.wait("vector", sPE, zmm)
        zc = P.op("vector", sDVE, lambda v: v.tensor_copy(out=zb[:, :],
                                                          in_=pf[4][:, 0:96]))
        P.wait("tensor", sDVE, zc)
        ztr = P.op("tensor", sPE, lambda t: t.transpose(
            pf[5][:, :].bitcast(BF16)[0:96, 0:128], zb[:, :], ident_b[:, :]))
        P.wait("vector", sPE, ztr)
        zc2 = P.op("vector", sDVE, lambda v: v.tensor_copy(
            out=zTt[:, :], in_=pf[5][:, :].bitcast(BF16)[0:96, 0:128]))
        zc2_prev[0] = zc2
        P.wait("tensor", sDVE, zc2)
        kdone = 0

        def emit_half(half):
            for ic in range(16):
                for j in range(4):
                    oc = half * 4 + j
                    P.opn("tensor", lambda t, ic=ic, j=j, oc=oc, lbuf=lbuf:
                          t.matmul(pf[j][:, :],
                                   lbuf[:, ic * 128:(ic + 1) * 128],
                                   wqkv_s[:, ic * 4096 + oc * 512:
                                          ic * 4096 + (oc + 1) * 512],
                                   start=(ic == 0), stop=False))
            hmm = 0
            for j in range(4):
                oc = half * 4 + j
                def mmfn(t, j=j, oc=oc):
                    return t.matmul(
                        pf[j][:, :], zTt[0:96, :],
                        bcomb_s[0:96, oc * 512:(oc + 1) * 512],
                        start=False, stop=True)
                if j == 3:
                    hmm = P.op("tensor", sPE, mmfn)
                else:
                    P.opn("tensor", mmfn)
            return hmm

        hmm0 = emit_half(0)
        P.wait("vector", sPE, hmm0)
        P.wait("vector", sACT)
        ec0 = 0
        for j in range(4):
            ec0 = P.op("vector", sDVE, lambda v, j=j: v.tensor_copy(
                out=qf4[:, j * 512:(j + 1) * 512], in_=pf[j][:, :]))
        P.wait("tensor", sDVE, ec0)
        hmm1 = emit_half(1)
        for oc in range(4):
            qdma_done = process_q(P, oc, tt, None)
        P.wait("vector", sPE, hmm1)
        P.wait("vector", sACT)
        P.wait("scalar", sPE, hmm1)
        for j in range(4):
            oc = 4 + j
            if oc < 6:
                ec = P.op("vector", sDVE, lambda v, j=j, oc=oc: v.tensor_copy(
                    out=qf4[:, (oc % 4) * 512:(oc % 4 + 1) * 512],
                    in_=pf[j][:, :]))
            else:
                P.wait("scalar", sG)
                ec = P.op("scalar", sACT, lambda s, j=j, oc=oc: s.activation(
                    out=vst[:, (oc - 6) * 512:(oc - 5) * 512],
                    in_=pf[j][:, :], func=AF.Copy))
                vst_prev[0] = ec
        for oc in (4, 5):
            kdone = process_q(P, oc, tt, None)
        P.wait("gpsimd", sDVE, kdone)
        P.wait("gpsimd", sACT)
        P.wait("gpsimd", sQSP)
        P.idma("gpsimd", kt_sh.ap(), IndirectOffsetOnAxis(kidx[:, tt:tt + 1], 0),
               ktst[:, :], None, sG)
        scat_done = P.idma("gpsimd", v_sh.ap(),
                           IndirectOffsetOnAxis(vidx[:, tt:tt + 1], 0),
                           vst[:, :], None, sG)
    P.wait("gpsimd", sG, scat_done)
    P.idma("gpsimd", flag_sh.ap(), IndirectOffsetOnAxis(fidx[:, :], 0),
           tokrow[:, :], None, sG)
    P.run()

    if stop == "qkv":
        return finisher()

    P = Ph(nc)
    mld = P.dma("sync", masks_s[:, :].rearrange("p (s c) -> p s c", c=512),
                masks_in.ap().rearrange("s p c -> p s c"), sONE)

    flagw = sG.c

    def spin(g):
        g.wait_ge(sG.h, flagw)
        tokA = nc.values_load(tokrow[0:1, 0:1], engines=[ET.Pool],
                              skip_runtime_bounds_check=True)
        tokB = nc.values_load(tokrow[1:2, 0:1], engines=[ET.Pool],
                              skip_runtime_bounds_check=True)
        nc.br("spin_loop", engines=[ET.Pool])
        with nc.body("spin_loop", valid_engines=[ET.Pool]):
            g.dma_start(out=flagv[:, :], in_=flag_sh.ap()).then_inc(sPoll.h, 16)
            g.nop(cycle_cnt=4000)
            v0 = nc.values_load(flagv[0:1, 0:1], engines=[ET.Pool],
                                skip_runtime_bounds_check=True)
            nc.br_cmp(v0, tokA, on_true="spin_c1", on_false="spin_loop",
                      comp_op="IS_EQ", engines=[ET.Pool])
        with nc.body("spin_c1", valid_engines=[ET.Pool]):
            v1 = nc.values_load(flagv[1:2, 0:1], engines=[ET.Pool],
                                skip_runtime_bounds_check=True)
            nc.br_cmp(v1, tokB, on_true="spin_c2", on_false="spin_loop",
                      comp_op="IS_EQ", engines=[ET.Pool])
        with nc.body("spin_c2", valid_engines=[ET.Pool]):
            v2 = nc.values_load(flagv[2:3, 0:1], engines=[ET.Pool],
                                skip_runtime_bounds_check=True)
            nc.br_cmp(v2, tokA, on_true="spin_c3", on_false="spin_loop",
                      comp_op="IS_EQ", engines=[ET.Pool])
        with nc.body("spin_c3", valid_engines=[ET.Pool]):
            v3 = nc.values_load(flagv[3:4, 0:1], engines=[ET.Pool],
                                skip_runtime_bounds_check=True)
            nc.br_cmp(v3, tokB, on_true="spin_done", on_false="spin_loop",
                      comp_op="IS_EQ", engines=[ET.Pool])
        nc.switch_bb("spin_done")
        g.nop().then_inc(sSpin.h, 1)
    if DEBUG_PRIVATE[0]:
        P.wait("gpsimd", sG)

        def spin2(g):
            g.nop().then_inc(sSpin.h, 1)
        P.L["gpsimd"].append(spin2)
        sSpin.c += 1
    else:
        P.L["gpsimd"].append(spin)
        sSpin.c += 1

    ktv = kt_sh.ap().rearrange("(dk g) (kvh tw) -> dk g kvh tw", g=32, tw=128)
    vtv = v_sh.ap().rearrange("(g p) (kvh dv) -> p g kvh dv", p=128, dv=128)

    seq = []
    for kvh in range(8):
        for qi in range(2):
            qh = 2 * kvh + qi
            for slot in range(16):
                seq.append(dict(kvh=kvh, qh=qh, slot=slot,
                                nch=NCH[slot], npad=NPAD[slot],
                                fk=(qi == 0 and slot == 0), fq=(slot == 0)))
    NS = len(seq)
    kl_tgt = [0] * 8
    vl_tgt = [0] * 8
    ql_tgt = [0] * 16

    def emit_kvh_load(kvh, war):
        if war is not None:
            P.wait("sync", sPE, war)
        kl_tgt[kvh] = P.dma(
            "sync", ktb[kvh % 2][:, :].rearrange("p (g t) -> p g t", t=128),
            ktv[:, :, kvh, :], sKT[kvh % 2])
        vl_tgt[kvh] = P.dma(
            "sync", vbuf[kvh % 2][:, :].rearrange("p (g d) -> p g d", d=128),
            vtv[:, :, kvh, :], sVB[kvh % 2])

    def emit_qh_load(qh, war):
        if war is not None:
            P.wait("sync", sPE, war)
        ql_tgt[qh] = P.dma(
            "sync", qtb[qh % 2][:, :].rearrange("p (s t) -> p s t", t=128),
            qtsp.ap()[qh, :, :, :].rearrange("s p t -> p s t"), sQT[qh % 2])

    P.wait("sync", sQSP)
    P.wait("vector", sONE)
    emit_qh_load(0, None)
    emit_qh_load(1, None)
    P.wait("sync", sSpin)
    emit_kvh_load(0, None)
    emit_kvh_load(1, None)

    s_cnt = [None] * NS
    evac_cnt = [None] * NS
    negmax_cnt = [0] * NS
    exp_cnt = [0] * NS
    recip_cnt = [0] * NS
    norm_cnt = [0] * NS
    t_end = [0] * NS
    av_end = [0] * NS
    y_cnt = [0] * NS
    ptc_bank = {6: 0, 7: 0}

    def emit_av_grp(j, g, ngrp, npad, kvh, avb, pt_sb, ptc, m):
        P.wait("tensor", sACT, ptc[g])
        if g == 0:
            if j >= 2:
                P.wait("tensor", sACT, y_cnt[j - 2])
            if m['fk']:
                P.wait("tensor", sVB[kvh % 2], vl_tgt[kvh])
        j0, j1 = g * 8, min(npad, g * 8 + 8)
        for jj in range(j0, j1):
            def mmfn(t, jj=jj, avb=avb, kvh=kvh, npad=npad, pt_sb=pt_sb):
                return t.matmul(pf[avb][:, 0:128],
                                vbuf[kvh % 2][:, jj * 128:(jj + 1) * 128],
                                pt_sb[:, jj * 128:(jj + 1) * 128],
                                start=(jj == 0), stop=(jj == npad - 1))
            if jj == npad - 1:
                av_end[j] = P.op("tensor", sPE, mmfn)
            else:
                P.opn("tensor", mmfn)

    for st in range(NS + 2):
        i = st if st < NS else None
        j = st - 2 if st >= 2 else None
        if st + 24 < NS and (st + 24) % 32 == 0:
            kvh_n = (st + 24) // 32
            if kvh_n >= 2:
                emit_kvh_load(kvh_n, av_end[st - 9])
        if st + 8 < NS and (st + 8) % 16 == 0:
            qh_n = (st + 8) // 16
            if qh_n >= 2:
                emit_qh_load(qh_n, s_cnt[st - 9][-1])
        if j is not None:
            m = seq[j]
            sp = j % 2
            cs_, cr = 25 + 4 * sp, 26 + 4 * sp
            P.wait("vector", sACT, exp_cnt[j])
            recip_cnt[j] = P.op("vector", sDVE, lambda v, cs_=cs_, cr=cr:
                                v.reciprocal(out=cols[:, cr:cr + 1],
                                             in_=cols[:, cs_:cs_ + 1]))
            npad, p_sb = m['npad'], p_sb2[sp]
            norm_cnt[j] = P.op("vector", sDVE, lambda v, npad=npad, p_sb=p_sb,
                               cr=cr: v.tensor_scalar(
                                   out=p_sb[:, 0:npad * 128],
                                   in0=p_sb[:, 0:npad * 128],
                                   scalar1=cols[:, cr:cr + 1], scalar2=None,
                                   op0=ALU.mult))
        e = st - 1
        if 0 <= e < NS:
            m = seq[e]
            sp = e % 2
            cm, cs_ = 24 + 4 * sp, 25 + 4 * sp
            nch = m['nch']
            s_sb, p_sb = s_sb2[sp], p_sb2[sp]
            P.wait("scalar", sDVE, negmax_cnt[e])
            if e - 2 >= 0:
                P.wait("scalar", sPE, t_end[e - 2])
                P.wait("scalar", sDVE, recip_cnt[e - 2])
            npad_e = m['npad']
            exp_cnt[e] = P.op("scalar", sACT, lambda q, npad_e=npad_e, s_sb=s_sb,
                              p_sb=p_sb, cm=cm, cs_=cs_: q.activation(
                                  out=p_sb[:, 0:npad_e * 128],
                                  in_=s_sb[:, 0:npad_e * 128],
                                  func=AF.Exp, bias=cols[:, cm:cm + 1],
                                  accum_out=cols[:, cs_:cs_ + 1]))
        if i is not None:
            m = seq[i]
            sp = i % 2
            cm = 24 + 4 * sp
            nch, slot, qh, kvh = m['nch'], m['slot'], m['qh'], m['kvh']
            s_sb = s_sb2[sp]
            base = 0 if sp == 0 else 4
            if m['fq']:
                P.wait("tensor", sQT[qh % 2], ql_tgt[qh])
            if m['fk']:
                P.wait("tensor", sKT[kvh % 2], kl_tgt[kvh])
            s_cnt[i] = []
            evac_cnt[i] = []
            for kc in range(nch):
                bank = base + (kc % 2)
                if kc >= 2:
                    P.wait("tensor", sDVE, evac_cnt[i][kc - 2])
                elif kc == 0 and i >= 2:
                    P.wait("tensor", sDVE, evac_cnt[i - 2][-1])
                s_cnt[i].append(P.op("tensor", sPE, lambda t, slot=slot, kc=kc,
                                     bank=bank, kvh=kvh, qh=qh: t.matmul(
                                         pf[bank][:, :],
                                         qtb[qh % 2][:, slot * 128:(slot + 1) * 128],
                                         ktb[kvh % 2][:, kc * 512:(kc + 1) * 512],
                                         start=True, stop=True)))
                if kc == 0 and i >= 2:
                    P.wait("vector", sACT, exp_cnt[i - 2])
                P.wait("vector", sPE, s_cnt[i][kc])
                if kc == nch - 1:
                    ev = P.op("vector", sDVE, lambda v, kc=kc, bank=bank,
                              s_sb=s_sb, slot=slot: v.tensor_tensor(
                                  out=s_sb[:, kc * 512:(kc + 1) * 512],
                                  in0=pf[bank][:, :],
                                  in1=masks_s[:, slot * 512:(slot + 1) * 512],
                                  op=ALU.add))
                else:
                    ev = P.op("vector", sDVE, lambda v, kc=kc, bank=bank,
                              s_sb=s_sb: v.tensor_copy(
                                  out=s_sb[:, kc * 512:(kc + 1) * 512],
                                  in_=pf[bank][:, :]))
                evac_cnt[i].append(ev)
            npad_i = m['npad']
            negmax_cnt[i] = P.op("vector", sDVE, lambda v, npad_i=npad_i,
                                 s_sb=s_sb, cm=cm: v.reduce_max(
                                     cols[:, cm:cm + 1], s_sb[:, 0:npad_i * 128],
                                     axis=AX.X, negate=True))
        if j is not None:
            m = seq[j]
            sp = j % 2
            npad, kvh, qh, slot = m['npad'], m['kvh'], m['qh'], m['slot']
            p_sb, pt_sb = p_sb2[sp], pt_sb2[sp]
            avb = 2 + sp
            ngrp = (npad + 7) // 8
            P.wait("tensor", sDVE, norm_cnt[j])
            ptc = [0] * ngrp
            for g in range(ngrp):
                pbank = 6 + (g % 2)
                pb = pf[pbank][:, :].bitcast(BF16)
                P.wait("tensor", sACT, ptc_bank[pbank])
                j0, j1 = g * 8, min(npad, g * 8 + 8)
                for jj in range(j0, j1):
                    def trfn(t, pb=pb, jj=jj, j0=j0, p_sb=p_sb):
                        return t.transpose(
                            pb[:, (jj - j0) * 128:(jj - j0 + 1) * 128],
                            p_sb[:, jj * 128:(jj + 1) * 128], ident_b[:, :])
                    if jj == j1 - 1:
                        tr = P.op("tensor", sPE, trfn)
                    else:
                        P.opn("tensor", trfn)
                P.wait("scalar", sPE, tr)
                if g == 0 and j >= 2:
                    P.wait("scalar", sPE, av_end[j - 2])
                ptc[g] = P.op("scalar", sACT, lambda q, pb=pb, j0=j0, j1=j1,
                              pt_sb=pt_sb: q.copy(
                                  out=pt_sb[:, j0 * 128:j1 * 128],
                                  in_=pb[:, 0:(j1 - j0) * 128]))
                ptc_bank[pbank] = ptc[g]
                if g >= 1:
                    emit_av_grp(j, g - 1, ngrp, npad, kvh, avb, pt_sb, ptc, m)
            emit_av_grp(j, ngrp - 1, ngrp, npad, kvh, avb, pt_sb, ptc, m)
            t_end[j] = tr
            P.wait("scalar", sPE, av_end[j])
            y_cnt[j] = P.op("scalar", sACT, lambda q, qh=qh, slot=slot, avb=avb:
                            q.activation(
                                out=yT_at[:, qh * 2048 + slot * 128:
                                          qh * 2048 + (slot + 1) * 128],
                                in_=pf[avb][:, 0:128], func=AF.Copy))
    P.run()

    if stop == "attn":
        return finisher()

    P = Ph(nc)
    wold = P.dma("gpsimd", wo_s[:, :].rearrange("p (ic o) -> p ic o", o=2048),
                 woT_in.ap().rearrange("(ic p) o -> p ic o", p=128), sONEG)
    aold = P.dma("gpsimd", aoT_s[:, :].rearrange("p (ic r) -> p ic r", r=32),
                 aoT_in.ap().rearrange("(ic p) r -> p ic r", p=128), sONEG)
    bold = P.dma("gpsimd", boT_s[:, :], boT_in.ap(), sONEG)
    P.wait("vector", sONEG)
    sofold = 0
    for ic in range(16):
        sofold = P.op("vector", sDVE, lambda v, ic=ic: v.tensor_scalar(
            out=wo_s[:, ic * 2048:(ic + 1) * 2048],
            in0=wo_s[:, ic * 2048:(ic + 1) * 2048],
            scalar1=sb_all[:, 3:4], scalar2=None, op0=ALU.mult))
    P.wait("tensor", sONEG)
    P.wait("tensor", sDVE, sofold)
    ln3_x2_done = 0
    x2dp = 0
    for tt in range(NTT):
        xb_ = x1rd[tt % 2]
        P.wait("scalar", sDVE)
        if tt == 0:
            P.wait("scalar", sSPILL)
        xl = P.dma("scalar", xb_[:, :], x1sp.ap()[tt * 128:(tt + 1) * 128, :],
                   sXR[tt % 2])
        for ic in range(16):
            def mmfn(t, ic=ic, tt=tt):
                return t.matmul(
                    pf[4][:, 0:32],
                    yT_at[:, ic * 2048 + tt * 128:ic * 2048 + (tt + 1) * 128],
                    aoT_s[:, ic * 32:(ic + 1) * 32],
                    start=(ic == 0), stop=(ic == 15))
            if ic == 15:
                zmm = P.op("tensor", sPE, mmfn)
            else:
                P.opn("tensor", mmfn)
        P.wait("vector", sPE, zmm)
        zc = P.op("vector", sDVE, lambda v: v.tensor_copy(out=zob[:, :],
                                                          in_=pf[4][:, 0:32]))

        def emit_main(ic0, ic1):
            for ic in range(ic0, ic1):
                for dc in range(4):
                    P.opn("tensor", lambda t, ic=ic, dc=dc, tt=tt: t.matmul(
                        pf[dc][:, :],
                        yT_at[:, ic * 2048 + tt * 128:ic * 2048 + (tt + 1) * 128],
                        wo_s[:, ic * 2048 + dc * 512:ic * 2048 + (dc + 1) * 512],
                        start=(ic == 0), stop=False))

        P.wait("tensor", sDVE, x2dp)
        emit_main(0, 8)
        P.wait("tensor", sDVE, zc)
        ztr = P.op("tensor", sPE, lambda t: t.transpose(
            pf[5][:, :].bitcast(BF16)[0:32, 0:128], zob[:, :], ident_b[:, :]))
        P.wait("vector", sPE, ztr)
        zc2 = P.op("vector", sDVE, lambda v: v.tensor_copy(
            out=zoT[:, :], in_=pf[5][:, :].bitcast(BF16)[0:32, 0:128]))
        emit_main(8, 16)
        P.wait("tensor", sDVE, zc2)
        for dc in range(4):
            def mmfn(t, dc=dc):
                return t.matmul(
                    pf[dc][:, :], zoT[0:32, :],
                    boT_s[0:32, dc * 512:(dc + 1) * 512],
                    start=False, stop=True)
            if dc == 3:
                wmm = P.op("tensor", sPE, mmfn)
            else:
                P.opn("tensor", mmfn)
        P.wait("vector", sPE, wmm)
        P.wait("vector", sXR[tt % 2], xl)
        P.wait("vector", sSPILL)
        P.wait("vector", sACT, ln3_x2_done)
        for dc in range(4):
            x2d = P.op("vector", sDVE, lambda v, dc=dc, xb_=xb_: v.tensor_tensor(
                out=x2t[:, dc * 512:(dc + 1) * 512], in0=pf[dc][:, :],
                in1=xb_[:, dc * 512:(dc + 1) * 512], op=ALU.add))
        x2dp = x2d
        P.wait("scalar", sDVE, x2d)
        P.dma("scalar", x2sp.ap()[tt * 128:(tt + 1) * 128, :], x2t[:, :], sSPILL)
        emit_ln_tile(P, x2t[:, :], lnb3, sqj3, lnx3T, tt)
        ln3_x2_done = sACT.c
    P.run()

    if stop == "wo":
        return finisher()

    P = Ph(nc)
    P.wait("sync", sSPILL)
    P.wait("gpsimd", sSPILL)
    P.wait("vector", sSPILL)
    P.wait("scalar", sSPILL)
    emit_ffn(P, lnx3T, x2sp, final=True, modes=("cast", "cast"))
    P.run()

    return nc


_NC_CACHE = {}


def _get_nc():
    if "nc" not in _NC_CACHE:
        _NC_CACHE["nc"] = build_nc()
    return _NC_CACHE["nc"]


def _host_inputs(inputs):
    x = np.asarray(inputs["L0_attn_x"], dtype=np.float32)
    Wq = np.asarray(inputs["L0_attn_W_q"], dtype=np.float32)
    Wk = np.asarray(inputs["L0_attn_W_k"], dtype=np.float32)
    Wv = np.asarray(inputs["L0_attn_W_v"], dtype=np.float32)
    Wo = np.asarray(inputs["L0_attn_W_o"], dtype=np.float32)
    Aq = np.asarray(inputs["L0_attn_A_q"], dtype=np.float32)
    Bq = np.asarray(inputs["L0_attn_B_q"], dtype=np.float32)
    Ak = np.asarray(inputs["L0_attn_A_k"], dtype=np.float32)
    Bk = np.asarray(inputs["L0_attn_B_k"], dtype=np.float32)
    Av = np.asarray(inputs["L0_attn_A_v"], dtype=np.float32)
    Bv = np.asarray(inputs["L0_attn_B_v"], dtype=np.float32)
    Ao = np.asarray(inputs["L0_attn_A_o"], dtype=np.float32)
    Bo = np.asarray(inputs["L0_attn_B_o"], dtype=np.float32)
    sq = float(np.asarray(inputs["L0_attn_s_q"]).reshape(-1)[0])
    sk = float(np.asarray(inputs["L0_attn_s_k"]).reshape(-1)[0])
    sv = float(np.asarray(inputs["L0_attn_s_v"]).reshape(-1)[0])
    so = float(np.asarray(inputs["L0_attn_s_o"]).reshape(-1)[0])
    qg = np.asarray(inputs["L0_attn_q_gain"], dtype=np.float32).reshape(16)
    wg = np.asarray(inputs["L0_ffn_w_gate"], dtype=np.float32)
    wu = np.asarray(inputs["L0_ffn_w_up"], dtype=np.float32)
    wd = np.asarray(inputs["L0_ffn_w_down"], dtype=np.float32)

    wgT = np.ascontiguousarray(
        wg.T.reshape(2048, 64, 128).transpose(1, 0, 2)).reshape(64 * 2048, 128)
    wuT = np.ascontiguousarray(
        wu.T.reshape(2048, 64, 128).transpose(1, 0, 2)).reshape(64 * 2048, 128)
    wdT = np.ascontiguousarray(wd.T)
    wqkvT = np.ascontiguousarray(np.concatenate([Wq.T, Wk.T, Wv.T], axis=1))
    woT = np.ascontiguousarray(Wo.T)
    acombT = np.ascontiguousarray(np.concatenate([Aq.T, Ak.T, Av.T], axis=1))
    bcombT = np.zeros((96, 2 * D), dtype=np.float32)
    bcombT[0:32, 0:2048] = Bq.T
    bcombT[32:64, 2048:3072] = Bk.T
    bcombT[64:96, 3072:4096] = Bv.T
    aoT = np.ascontiguousarray(Ao.T)
    boT = np.ascontiguousarray(Bo.T)
    svec = np.zeros((1, 32), dtype=np.float32)
    svec[0, 0:4] = [sq, sk, sv, so]
    svec[0, 4:20] = qg

    ident_b = np.eye(128, dtype=bfdt)
    ones_b = np.ones((1, 128), dtype=bfdt)

    j = np.arange(32, dtype=np.float64)
    inv_freq = 1.0 / (10000.0 ** (j / 32.0))
    pos = np.arange(4096, dtype=np.float64)
    ang = np.outer(pos, inv_freq)
    cosf = np.cos(ang).astype(np.float32)
    sinf = np.sin(ang).astype(np.float32)

    token = (int(time.time_ns()) & 0x7FFFFFF0) | 1
    tokA, tokB = token, token ^ 0x5A5A5A5A

    common = dict(wgT=wgT, wuT=wuT, wdT=wdT, wqkvT=wqkvT, woT=woT,
                  acombT=acombT, bcombT=bcombT, aoT=aoT, boT=boT, svec=svec,
                  ident_b=ident_b, ones_b=ones_b)

    in_maps = []
    for c in range(8):
        b, r = c // 2, c % 2
        G = ROLE_G[r]
        rows = np.concatenate([np.arange(g * 128, (g + 1) * 128) for g in G])
        xs = np.ascontiguousarray(x[b][rows])
        cos4 = np.ascontiguousarray(np.tile(cosf[rows], (1, 4)))
        sin4 = np.ascontiguousarray(np.tile(sinf[rows], (1, 4)))
        masks = np.zeros((16, 128, 512), dtype=np.float32)
        for s in range(16):
            g = G[s]
            kglob = (NCH[s] - 1) * 512 + np.arange(512)
            qglob = g * 128 + np.arange(128)
            masks[s] = np.where(kglob[None, :] <= qglob[:, None], 0.0, -1.0e9)
        masksb = masks.astype(bfdt)
        kidx = np.zeros((128, 16), dtype=np.uint32)
        vidx_ = np.zeros((128, 16), dtype=np.uint32)
        p = np.arange(128, dtype=np.uint32)
        for s in range(16):
            kidx[:, s] = p * 32 + G[s]
            vidx_[:, s] = G[s] * 128 + p
        fidx = (np.array([0, 1], dtype=np.uint32) + 2 * r).reshape(2, 1)
        tok = np.zeros((2, 64), dtype=np.uint32)
        tok[0, :] = tokA
        tok[1, :] = tokB
        in_maps.append(dict(common, x_in=xs, cos4=cos4, sin4=sin4, masks=masksb,
                            kidx=kidx, vidx=vidx_, fidx=fidx, tok=tok))
    return in_maps


def kernel(**inputs):
    in_maps = _host_inputs(inputs)
    nc = _get_nc()
    res = run_bass_kernel_spmd(nc, in_maps, core_ids=list(range(8)))
    out = np.zeros((4, 4096, 2048), dtype=np.float32)
    for c in range(8):
        b, r = c // 2, c % 2
        G = ROLE_G[r]
        shard = res.results[c]["y_out"]
        for s in range(16):
            out[b, G[s] * 128:(G[s] + 1) * 128, :] = shard[s * 128:(s + 1) * 128, :]
    return out
